# version 05
# baseline (speedup 1.0000x reference)
import numpy as np
import concourse.bass as bass
import concourse.mybir as mybir
from concourse.bass_utils import run_bass_kernel_spmd

F32 = mybir.dt.float32
AF = mybir.ActivationFunctionType
ALU = mybir.AluOpType
AX = mybir.AxisListType

B, L, DM = 4, 4096, 1024
DI = 2048
NST = 16
DTR = 64
E, II = 8, 4096
TOWN = 2048
TC = 256
NCH_PRE, NCH_OWN = 8, 8
FTC = 512
NFC = TOWN // FTC
EPS = 1e-5

ENGS = ("sync", "gpsimd", "tensor", "vector", "scalar")


class Sem:
    def __init__(self, h):
        self.h = h
        self.n = 0


def build_nc(nch_pre=NCH_PRE, nch_own=NCH_OWN, nfc=NFC, stage=99, dbg=False, nexp=E):
    import contextlib
    nc = bass.Bass("TRN2", target_bir_lowering=False, detect_race_conditions=False)
    stack = contextlib.ExitStack()

    d_in = {}
    def din(name, shape):
        d_in[name] = nc.dram_tensor(name, list(shape), F32, kind="ExternalInput").ap()
        return d_in[name]

    x_pre = din("x_pre", (TOWN, DM))
    x_own = din("x_own", (TOWN, DM))
    flag = din("flag", (128, 1))
    W_inT = din("W_inT", (DM, 2 * DI))
    W_xT = din("W_xT", (DI, 96))
    W_dtT = din("W_dtT", (DTR, DI))
    W_outT = din("W_outT", (DI, DM))
    convw = din("convw", (128, 64))
    convb = din("convb", (128, 16))
    bdt = din("bdt", (128, 16))
    alog = din("alog", (128, 256))
    dprm = din("dprm", (128, 16))
    rng_g = din("rng_g", (128, 8))
    rnb_g = din("rnb_g", (128, 8))
    rwT = din("rwT", (DM, E))
    rbias = din("rbias", (E, 1))
    lng = din("lng", (128, 64))
    lnb = din("lnb", (128, 64))
    w1 = din("w1", (E, DM, II))
    b1l = din("b1l", (128, 256))
    w2 = din("w2", (E, II, DM))
    b2m = din("b2m", (E, DM))
    ident = din("ident", (128, 128))
    sel = din("sel", (32, 32 * 128))

    out_d = nc.dram_tensor("out", [TOWN, DM], F32, kind="ExternalOutput").ap()
    _ik = "ExternalOutput" if dbg else "Internal"
    g_dbg = nc.dram_tensor("g_dbg", [128, 32 * FTC], F32, kind=_ik).ap()
    comb_dbg = nc.dram_tensor("comb_dbg", [128, 16 * 8], F32, kind=_ik).ap()
    lt_dbg = nc.dram_tensor("lt_dbg", [8, 256], F32, kind=_ik).ap()
    t8_dbg = nc.dram_tensor("t8_dbg", [128, 16], F32, kind=_ik).ap()
    hf_dbg = nc.dram_tensor("hf_dbg", [128, 8 * FTC], F32, kind=_ik).ap()
    xhat_d = nc.dram_tensor("xhat_d", [DM, TOWN], F32, kind=_ik).ap()
    x1_d = nc.dram_tensor("x1_d", [TOWN, DM], F32, kind=_ik).ap()

    sq = Sem(stack.enter_context(nc.semaphore("sq")))
    sp = Sem(stack.enter_context(nc.semaphore("sp")))
    sg = Sem(stack.enter_context(nc.semaphore("sg")))
    st = Sem(stack.enter_context(nc.semaphore("st")))
    sv = Sem(stack.enter_context(nc.semaphore("sv")))
    ss = Sem(stack.enter_context(nc.semaphore("ss")))

    ops = {e: [] for e in ENGS}

    def E_(eng, fn):
        ops[eng].append(fn)

    def W_(eng, sem, val=None):
        v = sem.n if val is None else val
        if v <= 0:
            return
        E_(eng, lambda q, h=sem.h, vv=v: q.wait_ge(h, vv))

    def I_(eng, sem, fn, k=1):
        sem.n += k
        E_(eng, lambda q, f=fn, h=sem.h, kk=k: f(q).then_inc(h, kk))

    def X_(eng, fn):
        E_(eng, fn)

    def N_(eng, sem, k=1):
        sem.n += k
        last = ops[eng].pop()
        ops[eng].append(lambda q, f=last, h=sem.h, kk=k: f(q).then_inc(h, kk))

    def dma(queue, sem, out_ap, in_ap):
        I_(queue, sem, lambda q, o=out_ap, i=in_ap: q.dma_start(out=o, in_=i), 16)
        return sem.n

    def sb(name, shape, dtype=F32):
        return stack.enter_context(nc.sbuf_tensor(name, list(shape), dtype))

    ident_s = sb("ident_s", (128, 128))
    sel_s = sb("sel_s", (32, 32 * 128))
    wxT_s = sb("wxT_s", (128, 16 * 96))
    wdtT_s = sb("wdtT_s", (64, DI))
    rwT_s = sb("rwT_s", (128, 64))
    abuf = sb("abuf", (128, 256))
    convw_s = sb("convw_s", (128, 64))
    convb_s = sb("convb_s", (128, 16))
    bdt_s = sb("bdt_s", (128, 16))
    dprm_s = sb("dprm_s", (128, 16))
    rng_s = sb("rng_s", (128, 8))
    rnb_s = sb("rnb_s", (128, 8))
    lng_s = sb("lng_s", (128, 64))
    lnb_s = sb("lnb_s", (128, 64))
    b1_s = sb("b1_s", (128, 256))
    b2_s = sb("b2_s", (E, DM))
    rb_s = sb("rb_s", (8, 1))
    flag_s = sb("flag_s", (128, 1))
    ones1 = sb("ones1", (1, 128))
    cst = sb("cst", (128, 2))
    comb_s = sb("comb_s", (128, 16 * 8))
    dbg_s = sb("dbg_s", (128, 16))

    pb = [stack.enter_context(nc.psum_tensor(f"pb{i}", [128, 512], F32)) for i in range(8)]

    for dst, src in [(ident_s, ident), (sel_s, sel), (wdtT_s, W_dtT),
                     (convw_s, convw), (convb_s, convb), (bdt_s, bdt),
                     (dprm_s, dprm), (rng_s, rng_g), (rnb_s, rnb_g),
                     (lng_s, lng), (lnb_s, lnb), (b1_s, b1l), (b2_s, b2m),
                     (rb_s, rbias), (flag_s, flag)]:
        dma("sync", sq, dst.ap(), src)
    for k in range(16):
        dma("sync", sq, wxT_s.ap()[:, k * 96:(k + 1) * 96], W_xT[k * 128:(k + 1) * 128, :])
    for k in range(8):
        dma("sync", sq, rwT_s.ap()[:, k * 8:(k + 1) * 8], rwT[k * 128:(k + 1) * 128, :])
    v_alog = dma("sync", sq, abuf.ap(), alog)
    consts_loaded = sq.n
    W_("scalar", sq, v_alog)
    I_("scalar", ss, lambda q: q.activation(out=abuf.ap(), in_=abuf.ap(), func=AF.Exp))
    W_("vector", ss)
    I_("vector", sv, lambda q: q.tensor_scalar_mul(out=abuf.ap(), in0=abuf.ap(), scalar1=-1.0))
    X_("vector", lambda q: q.memset(ones1.ap(), 1.0))
    X_("vector", lambda q: q.memset(cst.ap()[:, 0:1], EPS))
    I_("vector", sv, lambda q: q.memset(cst.ap()[:, 1:2], 1.0 / DM))
    abuf_ready = sv.n
    cst_ready = sv.n

    ARENA_W = 41900
    arena = sb("arena", (128, ARENA_W))

    class _Buf:
        __slots__ = ("_ap",)
        def __init__(self, ap_):
            self._ap = ap_
        def ap(self):
            return self._ap

    class _Bump:
        def __init__(self):
            self.off = 0
        def take(self, shape):
            assert shape[0] <= 128
            w = int(np.prod(shape[1:]))
            w_al = (w + 15) // 16 * 16
            ap_ = arena.ap()[0:shape[0], self.off:self.off + w]
            self.off += w_al
            assert self.off <= ARENA_W, f"arena overflow {self.off}"
            return _Buf(ap_)

    _m = _Bump()

    def msb(name, shape, dtype=F32):
        assert dtype is F32
        return _m.take(shape)

    xtm = msb("xtm", (128, 2 * DM))
    scr_tm = msb("scr_tm", (128, 2 * DM))
    x1tm = msb("x1tm", (128, 2 * DM))
    lnx_f = msb("lnx_f", (128, 8 * TC))
    wbuf = msb("wbuf", (128, 4 * 1024))
    xpb = msb("xpb", (128, 16 * 260))
    szb = msb("szb", (128, 16 * TC))
    ub = msb("ub", (128, 16 * TC))
    deltab = msb("deltab", (128, 16 * TC))
    dub = msb("dub", (128, 16 * TC))
    yac = msb("yac", (128, 16 * TC))
    xdbl = msb("xdbl", (64, TC))
    bcb = msb("bcb", (32, TC))
    Bb = msb("Bb", (128, TC))
    Cb = msb("Cb", (128, TC))
    dAb = msb("dAb", (128, 2 * TC))
    dBub = msb("dBub", (128, 4 * TC))
    sst = msb("sst", (128, 2 * TC))
    spacebuf = msb("spacebuf", (128, 4))
    tmpv = msb("tmpv", (128, TC))
    carry = msb("carry", (128, 256))
    sums = msb("sums", (128, 2))
    nmean = msb("nmean", (128, 2))
    ssq = msb("ssq", (128, 2))
    sdv = msb("sdv", (128, 2))
    rsv = msb("rsv", (128, 2))
    lt_f = msb("lt_f", (8, TC))
    lt_tm = msb("lt_tm", (128, 16))
    top8 = msb("top8", (128, 8))
    m1c = msb("m1c", (128, 2))
    m2c = msb("m2c", (128, 2))
    dcol = msb("dcol", (128, 2))
    w1c = msb("w1c", (128, 2))
    w2c = msb("w2c", (128, 2))
    w12 = msb("w12", (128, 1))
    ge1 = msb("ge1", (128, 16))
    ge2 = msb("ge2", (128, 16))

    W_("vector", sq, consts_loaded)
    X_("vector", lambda q: q.memset(carry.ap(), 0.0))
    xpb3 = xpb.ap().rearrange("p (c w) -> p c w", w=260)
    X_("vector", lambda q: q.memset(xpb3[:, :, 0:3], 0.0))
    I_("vector", sv, lambda q: q.memset(yac.ap(), 0.0))

    wbuf_free = [0, 0, 0, 0]
    dA_free = [abuf_ready, abuf_ready]
    xtm_free = 0
    x1_spill = 0
    xhat_spill = 0
    prev_y_done = 0

    for ch in range(nch_pre + nch_own):
        own = ch >= nch_pre
        t0 = (ch - nch_pre) * TC if own else (ch * TC) % TOWN
        xsrc = x_own if own else x_pre

        if ch == nch_pre:
            W_("vector", sq, consts_loaded)
            X_("vector", lambda q: q.tensor_scalar_mul(out=carry.ap(), in0=carry.ap(),
                                                       scalar1=flag_s.ap()))
            tails = xpb3[:, :, 0:3]
            I_("vector", sv, lambda q, t=tails: q.tensor_scalar_mul(out=t, in0=t,
                                                                    scalar1=flag_s.ap()))

        if stage < 2:
            continue
        W_("sync", sv, xtm_free)
        vx = 0
        for j in range(2):
            vx = dma("sync", sq, xtm.ap()[:, j * DM:(j + 1) * DM],
                     xsrc[t0 + j * 128: t0 + (j + 1) * 128, :])

        W_("scalar", sq, vx)
        for j in range(2):
            X_("scalar", lambda q, j=j: q.activation(
                out=scr_tm.ap()[:, j * DM:(j + 1) * DM],
                in_=xtm.ap()[:, j * DM:(j + 1) * DM], func=AF.Identity,
                accum_out=sums.ap()[:, j:j + 1]))
        N_("scalar", ss)
        W_("vector", ss)
        I_("vector", sv, lambda q: q.tensor_scalar_mul(out=nmean.ap(), in0=sums.ap(),
                                                       scalar1=-1.0 / DM))
        W_("scalar", sv)
        for j in range(2):
            X_("scalar", lambda q, j=j: q.activation(
                out=scr_tm.ap()[:, j * DM:(j + 1) * DM],
                in_=xtm.ap()[:, j * DM:(j + 1) * DM], func=AF.Square,
                bias=nmean.ap()[:, j:j + 1], accum_out=ssq.ap()[:, j:j + 1]))
        W_("scalar", sv, cst_ready)
        I_("scalar", ss, lambda q: q.activation(out=sdv.ap(), in_=ssq.ap(), func=AF.Sqrt,
                                                scale=cst.ap()[:, 1:2], bias=cst.ap()[:, 0:1]))
        W_("vector", ss)
        X_("vector", lambda q: q.reciprocal(out=rsv.ap(), in_=sdv.ap()))
        X_("vector", lambda q: q.tensor_copy(out=spacebuf.ap()[:, 0:2], in_=sdv.ap()))
        for j in range(2):
            X_("vector", lambda q, j=j: q.tensor_scalar(
                out=scr_tm.ap()[:, j * DM:(j + 1) * DM],
                in0=xtm.ap()[:, j * DM:(j + 1) * DM],
                scalar1=nmean.ap()[:, j:j + 1], scalar2=rsv.ap()[:, j:j + 1],
                op0=ALU.add, op1=ALU.mult))
        N_("vector", sv)
        lnx_tm_done = sv.n

        if not own:
            xtm_free = lnx_tm_done

        if stage < 3:
            continue
        W_("vector", sg, xhat_spill)
        W_("tensor", sv, lnx_tm_done)
        W_("tensor", sq, consts_loaded)
        for j in range(2):
            for b_ in range(8):
                bank = pb[(j * 8 + b_) % 2]
                if j * 8 + b_ >= 2:
                    W_("tensor", sv)
                I_("tensor", st, lambda q, j=j, b_=b_, bank=bank: q.transpose(
                    bank.ap()[:, 0:128], scr_tm.ap()[:, j * DM + b_ * 128: j * DM + (b_ + 1) * 128],
                    ident_s.ap()))
                W_("vector", st)
                I_("vector", sv, lambda q, j=j, b_=b_, bank=bank: q.tensor_copy(
                    out=lnx_f.ap()[:, b_ * TC + j * 128: b_ * TC + (j + 1) * 128],
                    in_=bank.ap()[:, 0:128]))
        lnxf_done = sv.n

        if stage < 4:
            continue
        n_mg = 4 if own else 2
        W_("tensor", sv, lnxf_done)
        for mg in range(n_mg):
            for k in range(8):
                slot = (mg * 8 + k) % 4
                W_("sync", st, wbuf_free[slot])
                vw = dma("sync", sq, wbuf.ap()[:, slot * 1024:(slot + 1) * 1024],
                         W_inT[k * 128:(k + 1) * 128, mg * 1024:(mg + 1) * 1024])
                W_("tensor", sq, vw)
                for m in range(8):
                    X_("tensor", lambda q, slot=slot, m=m, k=k: q.matmul(
                        pb[m].ap()[:, 0:TC],
                        wbuf.ap()[:, slot * 1024 + m * 128: slot * 1024 + (m + 1) * 128],
                        lnx_f.ap()[:, k * TC:(k + 1) * TC],
                        start=(k == 0), stop=(k == 7)))
                N_("tensor", st)
                wbuf_free[slot] = st.n
            mm_done = st.n
            if mg < 2:
                W_("vector", st, mm_done)
                for m in range(8):
                    dc = mg * 8 + m
                    X_("vector", lambda q, m=m, dc=dc: q.tensor_copy(
                        out=xpb.ap()[:, dc * 260 + 3: dc * 260 + 3 + TC],
                        in_=pb[m].ap()[:, 0:TC]))
                N_("vector", sv)
                W_("tensor", sv)
            else:
                W_("scalar", sv, prev_y_done)
                W_("scalar", st, mm_done)
                for m in range(8):
                    zc = (mg - 2) * 8 + m
                    X_("scalar", lambda q, m=m, zc=zc: q.activation(
                        out=szb.ap()[:, zc * TC:(zc + 1) * TC], in_=pb[m].ap()[:, 0:TC],
                        func=AF.Silu))
                N_("scalar", ss)
                W_("tensor", ss)

        if stage < 5:
            continue
        W_("vector", st)
        for dc in range(16):
            X_("vector", lambda q, dc=dc: q.tensor_scalar_mul(
                out=ub.ap()[:, dc * TC:(dc + 1) * TC],
                in0=xpb.ap()[:, dc * 260 + 0: dc * 260 + TC],
                scalar1=convw_s.ap()[:, dc * 4: dc * 4 + 1]))
            for kk in range(1, 4):
                X_("vector", lambda q, dc=dc, kk=kk: q.scalar_tensor_tensor(
                    out=ub.ap()[:, dc * TC:(dc + 1) * TC],
                    in0=xpb.ap()[:, dc * 260 + kk: dc * 260 + kk + TC],
                    scalar=convw_s.ap()[:, dc * 4 + kk: dc * 4 + kk + 1],
                    in1=ub.ap()[:, dc * TC:(dc + 1) * TC],
                    op0=ALU.mult, op1=ALU.add))
        I_("vector", sv, lambda q: q.tensor_copy(out=xpb3[:, :, 0:3], in_=xpb3[:, :, 256:259]))
        conv_done = sv.n
        W_("scalar", sv, conv_done)
        for dc in range(16):
            X_("scalar", lambda q, dc=dc: q.activation(
                out=ub.ap()[:, dc * TC:(dc + 1) * TC], in_=ub.ap()[:, dc * TC:(dc + 1) * TC],
                func=AF.Silu, bias=convb_s.ap()[:, dc:dc + 1]))
        N_("scalar", ss)
        u_done = ss.n

        if stage < 6:
            continue
        W_("tensor", ss, u_done)
        for k in range(16):
            X_("tensor", lambda q, k=k: q.matmul(
                pb[0].ap()[0:64, 0:TC], wxT_s.ap()[:, k * 96: k * 96 + 64],
                ub.ap()[:, k * TC:(k + 1) * TC], start=(k == 0), stop=(k == 15)))
        for k in range(16):
            X_("tensor", lambda q, k=k: q.matmul(
                pb[1].ap()[0:32, 0:TC], wxT_s.ap()[:, k * 96 + 64: k * 96 + 96],
                ub.ap()[:, k * TC:(k + 1) * TC], start=(k == 0), stop=(k == 15)))
        N_("tensor", st)
        W_("vector", st)
        X_("vector", lambda q: q.tensor_copy(out=xdbl.ap(), in_=pb[0].ap()[0:64, 0:TC]))
        I_("vector", sv, lambda q: q.tensor_copy(out=bcb.ap(), in_=pb[1].ap()[0:32, 0:TC]))
        xdbl_done = sv.n

        if stage < 7:
            continue
        W_("tensor", sv, xdbl_done)
        for m in range(16):
            bank = pb[2 + (m % 2)]
            if m >= 2:
                W_("tensor", ss)
            X_("tensor", lambda q, m=m, bank=bank: q.matmul(
                bank.ap()[:, 0:TC], wdtT_s.ap()[:, m * 128:(m + 1) * 128],
                xdbl.ap(), start=True, stop=True))
            N_("tensor", st)
            W_("scalar", st)
            X_("scalar", lambda q, m=m, bank=bank: q.activation(
                out=deltab.ap()[:, m * TC:(m + 1) * TC], in_=bank.ap()[:, 0:TC],
                func=AF.Exp, bias=bdt_s.ap()[:, m:m + 1]))
            I_("scalar", ss, lambda q, m=m: q.activation(
                out=deltab.ap()[:, m * TC:(m + 1) * TC],
                in_=deltab.ap()[:, m * TC:(m + 1) * TC], func=AF.Ln, bias=1.0))
        delta_done = ss.n

        if stage < 8:
            continue
        W_("vector", ss, delta_done)
        for dc in range(16):
            X_("vector", lambda q, dc=dc: q.tensor_mul(
                out=dub.ap()[:, dc * TC:(dc + 1) * TC],
                in0=deltab.ap()[:, dc * TC:(dc + 1) * TC],
                in1=ub.ap()[:, dc * TC:(dc + 1) * TC]))
        N_("vector", sv)
        du_done = sv.n

        pend_carry = None
        group_inc = {}
        gidx = 0
        for n in range(NST):
            W_("tensor", sv, du_done)
            if n >= 1:
                W_("tensor", sv)
            X_("tensor", lambda q, n=n: q.matmul(
                pb[4].ap()[:, 0:TC], sel_s.ap()[:, n * 128:(n + 1) * 128], bcb.ap(),
                start=True, stop=True))
            if own:
                X_("tensor", lambda q, n=n: q.matmul(
                    pb[5].ap()[:, 0:TC], sel_s.ap()[:, (16 + n) * 128:(17 + n) * 128], bcb.ap(),
                    start=True, stop=True))
            N_("tensor", st)
            W_("vector", st)
            X_("vector", lambda q: q.tensor_copy(out=Bb.ap(), in_=pb[4].ap()[:, 0:TC]))
            if own:
                X_("vector", lambda q: q.tensor_copy(out=Cb.ap(), in_=pb[5].ap()[:, 0:TC]))
            N_("vector", sv)
            bc_ready = sv.n

            for dc in range(16):
                slot = dc % 2
                W_("scalar", sv, dA_free[slot])
                col = dc * 16 + n
                X_("scalar", lambda q, dc=dc, col=col, slot=slot: q.activation(
                    out=dAb.ap()[:, slot * TC:(slot + 1) * TC],
                    in_=deltab.ap()[:, dc * TC:(dc + 1) * TC], func=AF.Exp,
                    scale=abuf.ap()[:, col:col + 1]))
                N_("scalar", ss)
                dA_ready = ss.n
                rs_ = gidx % 4
                if gidx - 4 in group_inc:
                    W_("gpsimd", sv, group_inc[gidx - 4])
                W_("gpsimd", sv, bc_ready)
                I_("gpsimd", sp, lambda q, dc=dc, rs_=rs_: q.tensor_tensor(
                    out=dBub.ap()[:, rs_ * TC:(rs_ + 1) * TC],
                    in0=dub.ap()[:, dc * TC:(dc + 1) * TC], in1=Bb.ap(), op=ALU.mult))
                dbu_ready = sp.n
                W_("vector", ss, dA_ready)
                W_("vector", sp, dbu_ready)
                sslot = (n * 16 + dc) % 2
                if pend_carry is not None:
                    pc, ps_ = pend_carry
                    X_("vector", lambda q, pc=pc, ps_=ps_: q.tensor_copy(
                        out=carry.ap()[:, pc:pc + 1],
                        in_=sst.ap()[:, ps_ * TC + TC - 1: ps_ * TC + TC]))
                X_("vector", lambda q, slot=slot, col=col, sslot=sslot, rs_=rs_: q.tensor_tensor_scan(
                    out=sst.ap()[:, sslot * TC:(sslot + 1) * TC],
                    data0=dAb.ap()[:, slot * TC:(slot + 1) * TC],
                    data1=dBub.ap()[:, rs_ * TC:(rs_ + 1) * TC],
                    initial=carry.ap()[:, col:col + 1],
                    op0=ALU.mult, op1=ALU.add))
                pend_carry = (col, sslot)
                if own:
                    if n == 0:
                        X_("vector", lambda q, dc=dc, sslot=sslot: q.tensor_mul(
                            out=yac.ap()[:, dc * TC:(dc + 1) * TC],
                            in0=sst.ap()[:, sslot * TC:(sslot + 1) * TC], in1=Cb.ap()))
                    else:
                        X_("vector", lambda q, sslot=sslot: q.tensor_mul(
                            out=tmpv.ap(), in0=sst.ap()[:, sslot * TC:(sslot + 1) * TC],
                            in1=Cb.ap()))
                        X_("vector", lambda q, dc=dc: q.tensor_add(
                            out=yac.ap()[:, dc * TC:(dc + 1) * TC],
                            in0=yac.ap()[:, dc * TC:(dc + 1) * TC], in1=tmpv.ap()))
                N_("vector", sv)
                dA_free[slot] = sv.n
                group_inc[gidx] = sv.n
                gidx += 1

        if pend_carry is not None:
            X_("vector", lambda q: q.tensor_copy(out=spacebuf.ap(), in_=carry.ap()[:, 0:4]))
            pc, ps_ = pend_carry
            I_("vector", sv, lambda q, pc=pc, ps_=ps_: q.tensor_copy(
                out=carry.ap()[:, pc:pc + 1],
                in_=sst.ap()[:, ps_ * TC + TC - 1: ps_ * TC + TC]))

        if not own or stage < 9:
            continue

        for dc in range(16):
            X_("vector", lambda q, dc=dc: q.scalar_tensor_tensor(
                out=tmpv.ap(), in0=ub.ap()[:, dc * TC:(dc + 1) * TC],
                scalar=dprm_s.ap()[:, dc:dc + 1],
                in1=yac.ap()[:, dc * TC:(dc + 1) * TC], op0=ALU.mult, op1=ALU.add))
            X_("vector", lambda q, dc=dc: q.tensor_mul(
                out=yac.ap()[:, dc * TC:(dc + 1) * TC], in0=tmpv.ap(),
                in1=szb.ap()[:, dc * TC:(dc + 1) * TC]))
        N_("vector", sv)
        y_done = sv.n
        prev_y_done = y_done

        W_("tensor", sv, y_done)
        for k in range(16):
            slot = k % 4
            W_("sync", st, wbuf_free[slot])
            vw = dma("sync", sq, wbuf.ap()[:, slot * 1024:(slot + 1) * 1024],
                     W_outT[k * 128:(k + 1) * 128, :])
            W_("tensor", sq, vw)
            for ts in range(2):
                for nn in range(2):
                    X_("tensor", lambda q, k=k, ts=ts, nn=nn, slot=slot: q.matmul(
                        pb[ts * 2 + nn].ap(),
                        yac.ap()[:, k * TC + ts * 128: k * TC + (ts + 1) * 128],
                        wbuf.ap()[:, slot * 1024 + nn * 512: slot * 1024 + (nn + 1) * 512],
                        start=(k == 0), stop=(k == 15)))
            N_("tensor", st)
            wbuf_free[slot] = st.n
        wout_done = st.n
        W_("vector", st, wout_done)
        W_("vector", sg, x1_spill)
        for ts in range(2):
            for nn in range(2):
                X_("vector", lambda q, ts=ts, nn=nn: q.tensor_add(
                    out=x1tm.ap()[:, ts * DM + nn * 512: ts * DM + (nn + 1) * 512],
                    in0=xtm.ap()[:, ts * DM + nn * 512: ts * DM + (nn + 1) * 512],
                    in1=pb[ts * 2 + nn].ap()))
        N_("vector", sv)
        x1_done = sv.n
        xtm_free = x1_done
        W_("gpsimd", sv, x1_done)
        for j in range(2):
            x1_spill = dma("gpsimd", sg, x1_d[t0 + j * 128: t0 + (j + 1) * 128, :],
                           x1tm.ap()[:, j * DM:(j + 1) * DM])

        if stage < 10:
            continue
        W_("scalar", sv, x1_done)
        for j in range(2):
            X_("scalar", lambda q, j=j: q.activation(
                out=scr_tm.ap()[:, j * DM:(j + 1) * DM],
                in_=x1tm.ap()[:, j * DM:(j + 1) * DM], func=AF.Identity,
                accum_out=sums.ap()[:, j:j + 1]))
        N_("scalar", ss)
        W_("vector", ss)
        I_("vector", sv, lambda q: q.tensor_scalar_mul(out=nmean.ap(), in0=sums.ap(),
                                                       scalar1=-1.0 / DM))
        W_("scalar", sv)
        for j in range(2):
            X_("scalar", lambda q, j=j: q.activation(
                out=scr_tm.ap()[:, j * DM:(j + 1) * DM],
                in_=x1tm.ap()[:, j * DM:(j + 1) * DM], func=AF.Square,
                bias=nmean.ap()[:, j:j + 1], accum_out=ssq.ap()[:, j:j + 1]))
        W_("scalar", sv, cst_ready)
        I_("scalar", ss, lambda q: q.activation(out=sdv.ap(), in_=ssq.ap(), func=AF.Sqrt,
                                                scale=cst.ap()[:, 1:2], bias=cst.ap()[:, 0:1]))
        W_("vector", ss)
        X_("vector", lambda q: q.reciprocal(out=rsv.ap(), in_=sdv.ap()))
        X_("vector", lambda q: q.tensor_copy(out=spacebuf.ap()[:, 0:2], in_=sdv.ap()))
        for j in range(2):
            X_("vector", lambda q, j=j: q.tensor_scalar(
                out=scr_tm.ap()[:, j * DM:(j + 1) * DM], in0=x1tm.ap()[:, j * DM:(j + 1) * DM],
                scalar1=nmean.ap()[:, j:j + 1], scalar2=rsv.ap()[:, j:j + 1],
                op0=ALU.add, op1=ALU.mult))
        N_("vector", sv)
        xh2_done = sv.n
        W_("tensor", sv, xh2_done)
        for j in range(2):
            for b_ in range(8):
                bank = pb[(j * 8 + b_) % 2]
                if j * 8 + b_ >= 2:
                    W_("tensor", sv)
                I_("tensor", st, lambda q, j=j, b_=b_, bank=bank: q.transpose(
                    bank.ap()[:, 0:128], scr_tm.ap()[:, j * DM + b_ * 128: j * DM + (b_ + 1) * 128],
                    ident_s.ap()))
                W_("vector", st)
                I_("vector", sv, lambda q, j=j, b_=b_, bank=bank: q.tensor_copy(
                    out=lnx_f.ap()[:, b_ * TC + j * 128: b_ * TC + (j + 1) * 128],
                    in_=bank.ap()[:, 0:128]))
        xhf_done = sv.n
        W_("gpsimd", sv, xhf_done)
        for dc in range(8):
            xhat_spill = dma("gpsimd", sg, xhat_d[dc * 128:(dc + 1) * 128, t0: t0 + TC],
                             lnx_f.ap()[:, dc * TC:(dc + 1) * TC])

        if stage < 11:
            continue
        for dc in range(8):
            X_("vector", lambda q, dc=dc: q.tensor_scalar(
                out=scr_tm.ap()[:, dc * TC:(dc + 1) * TC],
                in0=lnx_f.ap()[:, dc * TC:(dc + 1) * TC],
                scalar1=rng_s.ap()[:, dc:dc + 1], scalar2=rnb_s.ap()[:, dc:dc + 1],
                op0=ALU.mult, op1=ALU.add))
        N_("vector", sv)
        normed_done = sv.n
        W_("tensor", sv, normed_done)
        for k in range(8):
            X_("tensor", lambda q, k=k: q.matmul(
                pb[6].ap()[0:8, 0:TC], rwT_s.ap()[:, k * 8:(k + 1) * 8],
                scr_tm.ap()[:, k * TC:(k + 1) * TC], start=(k == 0), stop=(k == 7)))
        N_("tensor", st)
        W_("vector", st)
        I_("vector", sv, lambda q: q.tensor_scalar_add(
            out=lt_f.ap(), in0=pb[6].ap()[0:8, 0:TC], scalar1=rb_s.ap()))
        logits_done = sv.n
        if dbg and ch == nch_pre:
            W_("gpsimd", sv, logits_done)
            dma("gpsimd", sg, lt_dbg, lt_f.ap())
        W_("tensor", sv, logits_done)
        for j in range(2):
            X_("tensor", lambda q, j=j: q.transpose(
                pb[7].ap()[:, j * 8:(j + 1) * 8], lt_f.ap()[:, j * 128:(j + 1) * 128],
                ident_s.ap()[0:8, 0:8]))
        N_("tensor", st)
        W_("vector", st)
        for j in range(2):
            X_("vector", lambda q, j=j: q.tensor_copy(
                out=lt_tm.ap()[:, j * 8:(j + 1) * 8], in_=pb[7].ap()[:, j * 8:(j + 1) * 8]))
        for j in range(2):
            X_("vector", lambda q, j=j: q.reduce_max(
                out=m1c.ap()[:, j:j + 1], in_=lt_tm.ap()[:, j * 8:(j + 1) * 8], axis=AX.X))
        for j in range(2):
            X_("vector", lambda q, j=j: q.tensor_single_scalar(
                out=ge1.ap()[:, j * 8:(j + 1) * 8], in_=lt_tm.ap()[:, j * 8:(j + 1) * 8],
                scalar=m1c.ap()[:, j:j + 1], op=ALU.is_ge))
        for j in range(2):
            X_("vector", lambda q, j=j: q.scalar_tensor_tensor(
                out=ge2.ap()[:, j * 8:(j + 1) * 8], in0=ge1.ap()[:, j * 8:(j + 1) * 8],
                scalar=-1e30, in1=lt_tm.ap()[:, j * 8:(j + 1) * 8],
                op0=ALU.mult, op1=ALU.add))
        for j in range(2):
            X_("vector", lambda q, j=j: q.reduce_max(
                out=m2c.ap()[:, j:j + 1], in_=ge2.ap()[:, j * 8:(j + 1) * 8], axis=AX.X))
        for j in range(2):
            X_("vector", lambda q, j=j: q.tensor_single_scalar(
                out=ge2.ap()[:, j * 8:(j + 1) * 8], in_=ge2.ap()[:, j * 8:(j + 1) * 8],
                scalar=m2c.ap()[:, j:j + 1], op=ALU.is_ge))
        X_("vector", lambda q: q.tensor_sub(out=dcol.ap(), in0=m1c.ap(), in1=m2c.ap()))
        N_("vector", sv)
        if dbg and ch == nch_pre:
            X_("vector", lambda q: q.tensor_copy(out=dbg_s.ap()[:, 0:8], in_=lt_tm.ap()[:, 0:8]))
            X_("vector", lambda q: q.tensor_copy(out=dbg_s.ap()[:, 8:9], in_=m1c.ap()[:, 0:1]))
            I_("vector", sv, lambda q: q.tensor_copy(out=dbg_s.ap()[:, 9:10], in_=m2c.ap()[:, 0:1]))
            W_("gpsimd", sv)
            dma("gpsimd", sg, t8_dbg, dbg_s.ap())
        W_("scalar", sv)
        X_("scalar", lambda q: q.activation(out=w1c.ap(), in_=dcol.ap(), func=AF.Sigmoid))
        X_("scalar", lambda q: q.activation(out=w2c.ap(), in_=dcol.ap(), func=AF.Sigmoid,
                                            scale=-1.0))
        N_("scalar", ss)
        W_("vector", ss)
        for j in range(2):
            tg = (ch - nch_pre) * 2 + j
            X_("vector", lambda q, tg=tg, j=j: q.tensor_scalar_mul(
                out=comb_s.ap()[:, tg * 8:(tg + 1) * 8], in0=ge1.ap()[:, j * 8:(j + 1) * 8],
                scalar1=w1c.ap()[:, j:j + 1]))
        for j in range(2):
            tg = (ch - nch_pre) * 2 + j
            X_("vector", lambda q, tg=tg, j=j: q.scalar_tensor_tensor(
                out=comb_s.ap()[:, tg * 8:(tg + 1) * 8], in0=ge2.ap()[:, j * 8:(j + 1) * 8],
                scalar=w2c.ap()[:, j:j + 1],
                in1=comb_s.ap()[:, tg * 8:(tg + 1) * 8], op0=ALU.mult, op1=ALU.add))
        N_("vector", sv)

    if dbg:
        W_("gpsimd", sv)
        dma("gpsimd", sg, comb_dbg, comb_s.ap())
    bar_t, bar_v, bar_s = st.n, sv.n, ss.n
    bar_q, bar_g = sq.n, sg.n
    for eng in ENGS:
        W_(eng, st, bar_t)
        W_(eng, sv, bar_v)
        W_(eng, ss, bar_s)
        W_(eng, sq, bar_q)
        W_(eng, sg, bar_g)

    _f = _Bump()

    def fsb(name, shape, dtype=F32):
        assert dtype is F32
        return _f.take(shape)

    xh_f = fsb("xh_f", (128, 8 * FTC))
    h_f = fsb("h_f", (128, 8 * FTC))
    g_f = fsb("g_f", (128, 32 * FTC))
    x1f = fsb("x1f", (128, 4 * 1024))
    wbuf1 = fsb("wbuf1", (128, 4 * 512))
    wbuf2 = fsb("wbuf2", (128, 4 * 1024))
    tmpf = fsb("tmpf", (128, 512))

    w1_free = [0, 0, 0, 0]
    w2_free = [0, 0, 0, 0]
    evac_done = [0]
    st_out_total = 0
    xh_free = 0
    x1f_free = 0

    for fc in range(nfc if stage >= 12 else 0):
        ft0 = fc * FTC
        W_("sync", sv, xh_free)
        W_("sync", sg, x1f_free)
        vxh = 0
        for dc in range(8):
            vxh = dma("sync", sq, xh_f.ap()[:, dc * FTC:(dc + 1) * FTC],
                      xhat_d[dc * 128:(dc + 1) * 128, ft0: ft0 + FTC])
        vx1 = 0
        for ts in range(4):
            vx1 = dma("sync", sq, x1f.ap()[:, ts * 1024:(ts + 1) * 1024],
                      x1_d[ft0 + ts * 128: ft0 + (ts + 1) * 128, :])
        W_("vector", sq, vx1)

        for e in range(nexp):
            if e == 0:
                W_("vector", sq, vxh)
                W_("vector", st)
                for dc in range(8):
                    X_("vector", lambda q, dc=dc: q.tensor_scalar(
                        out=h_f.ap()[:, dc * FTC:(dc + 1) * FTC],
                        in0=xh_f.ap()[:, dc * FTC:(dc + 1) * FTC],
                        scalar1=lng_s.ap()[:, dc: dc + 1],
                        scalar2=lnb_s.ap()[:, dc: dc + 1],
                        op0=ALU.mult, op1=ALU.add))
                N_("vector", sv)
                hf_done = sv.n
            else:
                hf_done = hf_next_done

            if stage < 13:
                continue
            W_("tensor", sv, hf_done)
            W_("tensor", sv, evac_done[0])
            for mgh in range(8):
                bb = (mgh % 2) * 4
                if mgh > 1:
                    W_("tensor", ss)
                for k in range(8):
                    slot = (mgh * 8 + k) % 4
                    W_("sync", st, w1_free[slot])
                    vw = dma("sync", sq, wbuf1.ap()[:, slot * 512:(slot + 1) * 512],
                             w1[e, k * 128:(k + 1) * 128, mgh * 512:(mgh + 1) * 512])
                    W_("tensor", sq, vw)
                    for m in range(4):
                        X_("tensor", lambda q, slot=slot, m=m, k=k, bb=bb: q.matmul(
                            pb[bb + m].ap(),
                            wbuf1.ap()[:, slot * 512 + m * 128: slot * 512 + (m + 1) * 128],
                            h_f.ap()[:, k * FTC:(k + 1) * FTC],
                            start=(k == 0), stop=(k == 7)))
                    N_("tensor", st)
                    w1_free[slot] = st.n
                mg_done = st.n
                W_("scalar", st, mg_done)
                for m in range(4):
                    ic = mgh * 4 + m
                    X_("scalar", lambda q, e=e, m=m, ic=ic, bb=bb: q.activation(
                        out=g_f.ap()[:, ic * FTC:(ic + 1) * FTC], in_=pb[bb + m].ap(),
                        func=AF.Gelu, bias=b1_s.ap()[:, e * 32 + ic: e * 32 + ic + 1]))
                N_("scalar", ss)
            g_done = ss.n
            w1_all_done = st.n
            if dbg and fc == 0 and e == 0:
                W_("gpsimd", ss, g_done)
                dma("gpsimd", sg, g_dbg, g_f.ap())
                W_("gpsimd", sv, hf_done)
                dma("gpsimd", sg, hf_dbg, h_f.ap())

            if e + 1 < nexp:
                W_("vector", st, w1_all_done)
                for dc in range(8):
                    X_("vector", lambda q, e=e, dc=dc: q.tensor_scalar(
                        out=h_f.ap()[:, dc * FTC:(dc + 1) * FTC],
                        in0=xh_f.ap()[:, dc * FTC:(dc + 1) * FTC],
                        scalar1=lng_s.ap()[:, (e + 1) * 8 + dc: (e + 1) * 8 + dc + 1],
                        scalar2=lnb_s.ap()[:, (e + 1) * 8 + dc: (e + 1) * 8 + dc + 1],
                        op0=ALU.mult, op1=ALU.add))
                N_("vector", sv)
                hf_next_done = sv.n

            if stage < 14:
                continue
            W_("tensor", ss, g_done)
            if e > 0 or fc > 0:
                W_("vector", sv)
            for ts in range(4):
                for nn in range(2):
                    X_("tensor", lambda q, e=e, ts=ts, nn=nn: q.matmul(
                        pb[ts * 2 + nn].ap(), sel_s.ap()[0:8, e * 128:(e + 1) * 128],
                        b2_s.ap()[:, nn * 512:(nn + 1) * 512],
                        start=True, stop=False))
            for k2 in range(32):
                slot = k2 % 4
                W_("sync", st, w2_free[slot])
                vw = dma("sync", sq, wbuf2.ap()[:, slot * 1024:(slot + 1) * 1024],
                         w2[e, k2 * 128:(k2 + 1) * 128, :])
                W_("tensor", sq, vw)
                for ts in range(4):
                    for nn in range(2):
                        X_("tensor", lambda q, k2=k2, ts=ts, nn=nn, slot=slot: q.matmul(
                            pb[ts * 2 + nn].ap(),
                            g_f.ap()[:, k2 * FTC + ts * 128: k2 * FTC + (ts + 1) * 128],
                            wbuf2.ap()[:, slot * 1024 + nn * 512: slot * 1024 + (nn + 1) * 512],
                            start=False, stop=(k2 == 31)))
                N_("tensor", st)
                w2_free[slot] = st.n
            w2_done = st.n
            W_("vector", st, w2_done)
            for ts in range(4):
                tg = fc * 4 + ts
                for nn in range(2):
                    X_("vector", lambda q, e=e, ts=ts, nn=nn, tg=tg: q.tensor_scalar_mul(
                        out=tmpf.ap(), in0=pb[ts * 2 + nn].ap(),
                        scalar1=comb_s.ap()[:, tg * 8 + e: tg * 8 + e + 1]))
                    X_("vector", lambda q, ts=ts, nn=nn: q.tensor_add(
                        out=x1f.ap()[:, ts * 1024 + nn * 512: ts * 1024 + (nn + 1) * 512],
                        in0=x1f.ap()[:, ts * 1024 + nn * 512: ts * 1024 + (nn + 1) * 512],
                        in1=tmpf.ap()))
            N_("vector", sv)
            evac_done[0] = sv.n

        out_ready = sv.n
        xh_free = out_ready
        W_("gpsimd", sv, out_ready)
        for ts in range(4):
            st_out_total = dma("gpsimd", sg, out_d[ft0 + ts * 128: ft0 + (ts + 1) * 128, :],
                               x1f.ap()[:, ts * 1024:(ts + 1) * 1024])
        x1f_free = st_out_total

    if st_out_total == 0:
        for ts in range(4):
            st_out_total = dma("gpsimd", sg, out_d[ts * 128:(ts + 1) * 128, :],
                               arena.ap()[:, 0:1024])
    W_("gpsimd", sg, st_out_total)

    with nc.Block() as block:
        @block.sync
        def _(q):
            for f in ops["sync"]:
                f(q)

        @block.gpsimd
        def _(q):
            for f in ops["gpsimd"]:
                f(q)

        @block.tensor
        def _(q):
            for f in ops["tensor"]:
                f(q)

        @block.vector
        def _(q):
            for f in ops["vector"]:
                f(q)

        @block.scalar
        def _(q):
            for f in ops["scalar"]:
                f(q)

    nc.reset()
    stack.pop_all()
    return nc


def _host_layout(inputs):
    g = lambda k: np.ascontiguousarray(np.asarray(inputs[k], dtype=np.float32))
    x = g("L0_attn_x")
    W_in = g("L0_attn_W_in")
    conv_w = g("L0_attn_conv_w")
    conv_b = g("L0_attn_conv_b")
    W_x = g("L0_attn_W_x")
    W_dt = g("L0_attn_W_dt")
    b_dt = g("L0_attn_b_dt")
    A_log = g("L0_attn_A_log")
    D_param = g("L0_attn_D_param")
    W_out = g("L0_attn_W_out")
    rn_g = g("L0_ffn_rn_g")
    rn_b = g("L0_ffn_rn_b")
    router_w = g("L0_ffn_router_w")
    router_b = g("L0_ffn_router_b")
    ln_g = g("L0_ffn_ln_g")
    ln_b = g("L0_ffn_ln_b")
    w1 = g("L0_ffn_w1")
    b1 = g("L0_ffn_b1")
    w2 = g("L0_ffn_w2")
    b2 = g("L0_ffn_b2")

    C = np.ascontiguousarray
    shared = {
        "W_inT": C(W_in.T),
        "W_xT": C(W_x.T),
        "W_dtT": C(W_dt.T),
        "W_outT": C(W_out.T),
        "convw": C(conv_w[:, 0, :].reshape(16, 128, 4).transpose(1, 0, 2).reshape(128, 64)),
        "convb": C(conv_b.reshape(16, 128).T),
        "bdt": C(b_dt.reshape(16, 128).T),
        "alog": C(A_log.reshape(16, 128, 16).transpose(1, 0, 2).reshape(128, 256)),
        "dprm": C(D_param.reshape(16, 128).T),
        "rng_g": C(rn_g.reshape(8, 128).T),
        "rnb_g": C(rn_b.reshape(8, 128).T),
        "rwT": C(router_w.T),
        "rbias": C(router_b.reshape(8, 1)),
        "lng": C(ln_g.reshape(8, 8, 128).transpose(2, 0, 1).reshape(128, 64)),
        "lnb": C(ln_b.reshape(8, 8, 128).transpose(2, 0, 1).reshape(128, 64)),
        "w1": w1,
        "b1l": C(b1.reshape(8, 32, 128).transpose(2, 0, 1).reshape(128, 256)),
        "w2": w2,
        "b2m": b2,
        "ident": np.eye(128, dtype=np.float32),
        "sel": C(np.kron(np.eye(32, dtype=np.float32), np.ones((1, 128), np.float32))),
    }
    in_maps = []
    for c in range(8):
        b, h = c // 2, c % 2
        m = dict(shared)
        m["x_pre"] = C(x[b, 0:TOWN, :])
        m["x_own"] = C(x[b, h * TOWN:(h + 1) * TOWN, :])
        m["flag"] = np.full((128, 1), float(h), dtype=np.float32)
        in_maps.append(m)
    return in_maps


def kernel(**inputs):
    in_maps = _host_layout(inputs)
    nc = build_nc()
    res = run_bass_kernel_spmd(nc, in_maps, core_ids=list(range(8)))
    out = np.empty((B, L, DM), dtype=np.float32)
    for c in range(8):
        b, h = c // 2, c % 2
        out[b, h * TOWN:(h + 1) * TOWN, :] = res.results[c]["out"]
    return out


# version 06
# speedup vs baseline: 1.0014x; 1.0014x over previous
import numpy as np
import concourse.bass as bass
import concourse.mybir as mybir
from concourse.bass_utils import run_bass_kernel_spmd

F32 = mybir.dt.float32
AF = mybir.ActivationFunctionType
ALU = mybir.AluOpType
AX = mybir.AxisListType

B, L, DM = 4, 4096, 1024
DI = 2048
NST = 16
DTR = 64
E, II = 8, 4096
TOWN = 2048
TC = 256
NCH_PRE, NCH_OWN = 8, 8
FTC = 512
NFC = TOWN // FTC
EPS = 1e-5

ENGS = ("sync", "gpsimd", "tensor", "vector", "scalar")


class Sem:
    def __init__(self, h):
        self.h = h
        self.n = 0


def build_nc(nch_pre=NCH_PRE, nch_own=NCH_OWN, nfc=NFC, stage=99, dbg=False, nexp=E):
    import contextlib
    nc = bass.Bass("TRN2", target_bir_lowering=False, detect_race_conditions=False)
    stack = contextlib.ExitStack()

    d_in = {}
    def din(name, shape):
        d_in[name] = nc.dram_tensor(name, list(shape), F32, kind="ExternalInput").ap()
        return d_in[name]

    x_pre = din("x_pre", (TOWN, DM))
    x_own = din("x_own", (TOWN, DM))
    flag = din("flag", (128, 1))
    W_inT = din("W_inT", (DM, 2 * DI))
    W_xT = din("W_xT", (DI, 96))
    W_dtT = din("W_dtT", (DTR, DI))
    W_outT = din("W_outT", (DI, DM))
    convw = din("convw", (128, 64))
    convb = din("convb", (128, 16))
    bdt = din("bdt", (128, 16))
    alog = din("alog", (128, 256))
    dprm = din("dprm", (128, 16))
    rng_g = din("rng_g", (128, 8))
    rnb_g = din("rnb_g", (128, 8))
    rwT = din("rwT", (DM, E))
    rbias = din("rbias", (E, 1))
    lng = din("lng", (128, 64))
    lnb = din("lnb", (128, 64))
    w1 = din("w1", (E, DM, II))
    b1l = din("b1l", (128, 256))
    w2 = din("w2", (E, II, DM))
    b2m = din("b2m", (E, DM))
    ident = din("ident", (128, 128))
    sel = din("sel", (32, 32 * 128))

    out_d = nc.dram_tensor("out", [TOWN, DM], F32, kind="ExternalOutput").ap()
    _ik = "ExternalOutput" if dbg else "Internal"
    g_dbg = nc.dram_tensor("g_dbg", [128, 32 * FTC], F32, kind=_ik).ap()
    comb_dbg = nc.dram_tensor("comb_dbg", [128, 16 * 8], F32, kind=_ik).ap()
    lt_dbg = nc.dram_tensor("lt_dbg", [8, 256], F32, kind=_ik).ap()
    t8_dbg = nc.dram_tensor("t8_dbg", [128, 16], F32, kind=_ik).ap()
    hf_dbg = nc.dram_tensor("hf_dbg", [128, 8 * FTC], F32, kind=_ik).ap()
    xhat_d = nc.dram_tensor("xhat_d", [DM, TOWN], F32, kind=_ik).ap()
    x1_d = nc.dram_tensor("x1_d", [TOWN, DM], F32, kind=_ik).ap()

    sq = Sem(stack.enter_context(nc.semaphore("sq")))
    sp = Sem(stack.enter_context(nc.semaphore("sp")))
    sg = Sem(stack.enter_context(nc.semaphore("sg")))
    st = Sem(stack.enter_context(nc.semaphore("st")))
    sv = Sem(stack.enter_context(nc.semaphore("sv")))
    ss = Sem(stack.enter_context(nc.semaphore("ss")))

    ops = {e: [] for e in ENGS}

    def E_(eng, fn):
        ops[eng].append(fn)

    def W_(eng, sem, val=None):
        v = sem.n if val is None else val
        if v <= 0:
            return
        E_(eng, lambda q, h=sem.h, vv=v: q.wait_ge(h, vv))

    def I_(eng, sem, fn, k=1):
        sem.n += k
        E_(eng, lambda q, f=fn, h=sem.h, kk=k: f(q).then_inc(h, kk))

    def X_(eng, fn):
        E_(eng, fn)

    def N_(eng, sem, k=1):
        sem.n += k
        last = ops[eng].pop()
        ops[eng].append(lambda q, f=last, h=sem.h, kk=k: f(q).then_inc(h, kk))

    def dma(queue, sem, out_ap, in_ap):
        I_(queue, sem, lambda q, o=out_ap, i=in_ap: q.dma_start(out=o, in_=i), 16)
        return sem.n

    def sb(name, shape, dtype=F32):
        return stack.enter_context(nc.sbuf_tensor(name, list(shape), dtype))

    ident_s = sb("ident_s", (128, 128))
    sel_s = sb("sel_s", (32, 32 * 128))
    wxT_s = sb("wxT_s", (128, 16 * 96))
    wdtT_s = sb("wdtT_s", (64, DI))
    rwT_s = sb("rwT_s", (128, 64))
    abuf = sb("abuf", (128, 256))
    convw_s = sb("convw_s", (128, 64))
    convb_s = sb("convb_s", (128, 16))
    bdt_s = sb("bdt_s", (128, 16))
    dprm_s = sb("dprm_s", (128, 16))
    rng_s = sb("rng_s", (128, 8))
    rnb_s = sb("rnb_s", (128, 8))
    lng_s = sb("lng_s", (128, 64))
    lnb_s = sb("lnb_s", (128, 64))
    b1_s = sb("b1_s", (128, 256))
    b2_s = sb("b2_s", (E, DM))
    rb_s = sb("rb_s", (8, 1))
    flag_s = sb("flag_s", (128, 1))
    ones1 = sb("ones1", (1, 128))
    cst = sb("cst", (128, 2))
    comb_s = sb("comb_s", (128, 16 * 8))
    dbg_s = sb("dbg_s", (128, 16))

    pb = [stack.enter_context(nc.psum_tensor(f"pb{i}", [128, 512], F32)) for i in range(8)]

    for dst, src in [(ident_s, ident), (sel_s, sel), (wdtT_s, W_dtT),
                     (convw_s, convw), (convb_s, convb), (bdt_s, bdt),
                     (dprm_s, dprm), (rng_s, rng_g), (rnb_s, rnb_g),
                     (lng_s, lng), (lnb_s, lnb), (b1_s, b1l), (b2_s, b2m),
                     (rb_s, rbias), (flag_s, flag)]:
        dma("sync", sq, dst.ap(), src)
    for k in range(16):
        dma("sync", sq, wxT_s.ap()[:, k * 96:(k + 1) * 96], W_xT[k * 128:(k + 1) * 128, :])
    for k in range(8):
        dma("sync", sq, rwT_s.ap()[:, k * 8:(k + 1) * 8], rwT[k * 128:(k + 1) * 128, :])
    v_alog = dma("sync", sq, abuf.ap(), alog)
    consts_loaded = sq.n
    W_("scalar", sq, v_alog)
    I_("scalar", ss, lambda q: q.activation(out=abuf.ap(), in_=abuf.ap(), func=AF.Exp))
    W_("vector", ss)
    I_("vector", sv, lambda q: q.tensor_scalar_mul(out=abuf.ap(), in0=abuf.ap(), scalar1=-1.0))
    X_("vector", lambda q: q.memset(ones1.ap(), 1.0))
    X_("vector", lambda q: q.memset(cst.ap()[:, 0:1], EPS))
    I_("vector", sv, lambda q: q.memset(cst.ap()[:, 1:2], 1.0 / DM))
    abuf_ready = sv.n
    cst_ready = sv.n

    ARENA_W = 41900
    arena = sb("arena", (128, ARENA_W))

    class _Buf:
        __slots__ = ("_ap",)
        def __init__(self, ap_):
            self._ap = ap_
        def ap(self):
            return self._ap

    class _Bump:
        def __init__(self):
            self.off = 0
        def take(self, shape):
            assert shape[0] <= 128
            w = int(np.prod(shape[1:]))
            w_al = (w + 15) // 16 * 16
            ap_ = arena.ap()[0:shape[0], self.off:self.off + w]
            self.off += w_al
            assert self.off <= ARENA_W, f"arena overflow {self.off}"
            return _Buf(ap_)

    _m = _Bump()

    def msb(name, shape, dtype=F32):
        assert dtype is F32
        return _m.take(shape)

    xtm = msb("xtm", (128, 2 * DM))
    scr_tm = msb("scr_tm", (128, 2 * DM))
    x1tm = msb("x1tm", (128, 2 * DM))
    lnx_f = msb("lnx_f", (128, 8 * TC))
    wbuf = msb("wbuf", (128, 4 * 1024))
    xpb = msb("xpb", (128, 16 * 260))
    szb = msb("szb", (128, 16 * TC))
    ub = msb("ub", (128, 16 * TC))
    deltab = msb("deltab", (128, 16 * TC))
    dub = msb("dub", (128, 16 * TC))
    yac = msb("yac", (128, 16 * TC))
    xdbl = msb("xdbl", (64, TC))
    bcb = msb("bcb", (32, TC))
    Bb = msb("Bb", (128, TC))
    Cb = msb("Cb", (128, TC))
    dAb = msb("dAb", (128, 2 * TC))
    dBub = msb("dBub", (128, 4 * TC))
    sst = msb("sst", (128, 2 * TC))
    spacebuf = msb("spacebuf", (128, 4))
    tmpv = msb("tmpv", (128, TC))
    carry = msb("carry", (128, 256))
    sums = msb("sums", (128, 2))
    nmean = msb("nmean", (128, 2))
    ssq = msb("ssq", (128, 2))
    sdv = msb("sdv", (128, 2))
    rsv = msb("rsv", (128, 2))
    lt_f = msb("lt_f", (8, TC))
    lt_tm = msb("lt_tm", (128, 16))
    top8 = msb("top8", (128, 8))
    m1c = msb("m1c", (128, 2))
    m2c = msb("m2c", (128, 2))
    dcol = msb("dcol", (128, 2))
    w1c = msb("w1c", (128, 2))
    w2c = msb("w2c", (128, 2))
    w12 = msb("w12", (128, 1))
    ge1 = msb("ge1", (128, 16))
    ge2 = msb("ge2", (128, 16))

    W_("vector", sq, consts_loaded)
    X_("vector", lambda q: q.memset(carry.ap(), 0.0))
    xpb3 = xpb.ap().rearrange("p (c w) -> p c w", w=260)
    X_("vector", lambda q: q.memset(xpb3[:, :, 0:3], 0.0))
    I_("vector", sv, lambda q: q.memset(yac.ap(), 0.0))

    wbuf_free = [0, 0, 0, 0]
    dA_free = [abuf_ready, abuf_ready]
    xtm_free = 0
    x1_spill = 0
    xhat_spill = 0
    prev_y_done = 0

    for ch in range(nch_pre + nch_own):
        own = ch >= nch_pre
        t0 = (ch - nch_pre) * TC if own else (ch * TC) % TOWN
        xsrc = x_own if own else x_pre

        if ch == nch_pre:
            W_("vector", sq, consts_loaded)
            X_("vector", lambda q: q.tensor_scalar_mul(out=carry.ap(), in0=carry.ap(),
                                                       scalar1=flag_s.ap()))
            tails = xpb3[:, :, 0:3]
            I_("vector", sv, lambda q, t=tails: q.tensor_scalar_mul(out=t, in0=t,
                                                                    scalar1=flag_s.ap()))

        if stage < 2:
            continue
        W_("sync", sv, xtm_free)
        vx = 0
        for j in range(2):
            vx = dma("sync", sq, xtm.ap()[:, j * DM:(j + 1) * DM],
                     xsrc[t0 + j * 128: t0 + (j + 1) * 128, :])

        W_("scalar", sq, vx)
        for j in range(2):
            X_("scalar", lambda q, j=j: q.activation(
                out=scr_tm.ap()[:, j * DM:(j + 1) * DM],
                in_=xtm.ap()[:, j * DM:(j + 1) * DM], func=AF.Identity,
                accum_out=sums.ap()[:, j:j + 1]))
        N_("scalar", ss)
        W_("vector", ss)
        I_("vector", sv, lambda q: q.tensor_scalar_mul(out=nmean.ap(), in0=sums.ap(),
                                                       scalar1=-1.0 / DM))
        W_("scalar", sv)
        for j in range(2):
            X_("scalar", lambda q, j=j: q.activation(
                out=scr_tm.ap()[:, j * DM:(j + 1) * DM],
                in_=xtm.ap()[:, j * DM:(j + 1) * DM], func=AF.Square,
                bias=nmean.ap()[:, j:j + 1], accum_out=ssq.ap()[:, j:j + 1]))
        W_("scalar", sv, cst_ready)
        I_("scalar", ss, lambda q: q.activation(out=sdv.ap(), in_=ssq.ap(), func=AF.Sqrt,
                                                scale=cst.ap()[:, 1:2], bias=cst.ap()[:, 0:1]))
        W_("vector", ss)
        X_("vector", lambda q: q.reciprocal(out=rsv.ap(), in_=sdv.ap()))
        X_("vector", lambda q: q.tensor_copy(out=spacebuf.ap()[:, 0:2], in_=sdv.ap()))
        for j in range(2):
            X_("vector", lambda q, j=j: q.tensor_scalar(
                out=scr_tm.ap()[:, j * DM:(j + 1) * DM],
                in0=xtm.ap()[:, j * DM:(j + 1) * DM],
                scalar1=nmean.ap()[:, j:j + 1], scalar2=rsv.ap()[:, j:j + 1],
                op0=ALU.add, op1=ALU.mult))
        N_("vector", sv)
        lnx_tm_done = sv.n

        if not own:
            xtm_free = lnx_tm_done

        if stage < 3:
            continue
        W_("vector", sg, xhat_spill)
        W_("tensor", sv, lnx_tm_done)
        W_("tensor", sq, consts_loaded)
        for j in range(2):
            for b_ in range(8):
                bank = pb[(j * 8 + b_) % 2]
                if j * 8 + b_ >= 2:
                    W_("tensor", sv)
                I_("tensor", st, lambda q, j=j, b_=b_, bank=bank: q.transpose(
                    bank.ap()[:, 0:128], scr_tm.ap()[:, j * DM + b_ * 128: j * DM + (b_ + 1) * 128],
                    ident_s.ap()))
                W_("vector", st)
                I_("vector", sv, lambda q, j=j, b_=b_, bank=bank: q.tensor_copy(
                    out=lnx_f.ap()[:, b_ * TC + j * 128: b_ * TC + (j + 1) * 128],
                    in_=bank.ap()[:, 0:128]))
        lnxf_done = sv.n

        if stage < 4:
            continue
        n_mg = 4 if own else 2
        W_("tensor", sv, lnxf_done)
        for mg in range(n_mg):
            for k in range(8):
                slot = (mg * 8 + k) % 4
                W_("sync", st, wbuf_free[slot])
                vw = dma("sync", sq, wbuf.ap()[:, slot * 1024:(slot + 1) * 1024],
                         W_inT[k * 128:(k + 1) * 128, mg * 1024:(mg + 1) * 1024])
                W_("tensor", sq, vw)
                for m in range(8):
                    X_("tensor", lambda q, slot=slot, m=m, k=k: q.matmul(
                        pb[m].ap()[:, 0:TC],
                        wbuf.ap()[:, slot * 1024 + m * 128: slot * 1024 + (m + 1) * 128],
                        lnx_f.ap()[:, k * TC:(k + 1) * TC],
                        start=(k == 0), stop=(k == 7)))
                N_("tensor", st)
                wbuf_free[slot] = st.n
            mm_done = st.n
            if mg < 2:
                W_("vector", st, mm_done)
                for m in range(8):
                    dc = mg * 8 + m
                    X_("vector", lambda q, m=m, dc=dc: q.tensor_copy(
                        out=xpb.ap()[:, dc * 260 + 3: dc * 260 + 3 + TC],
                        in_=pb[m].ap()[:, 0:TC]))
                N_("vector", sv)
                W_("tensor", sv)
            else:
                W_("scalar", sv, prev_y_done)
                W_("scalar", st, mm_done)
                for m in range(8):
                    zc = (mg - 2) * 8 + m
                    X_("scalar", lambda q, m=m, zc=zc: q.activation(
                        out=szb.ap()[:, zc * TC:(zc + 1) * TC], in_=pb[m].ap()[:, 0:TC],
                        func=AF.Silu))
                N_("scalar", ss)
                W_("tensor", ss)

        if stage < 5:
            continue
        W_("vector", st)
        for dc in range(16):
            X_("vector", lambda q, dc=dc: q.tensor_scalar_mul(
                out=ub.ap()[:, dc * TC:(dc + 1) * TC],
                in0=xpb.ap()[:, dc * 260 + 0: dc * 260 + TC],
                scalar1=convw_s.ap()[:, dc * 4: dc * 4 + 1]))
            for kk in range(1, 4):
                X_("vector", lambda q, dc=dc, kk=kk: q.scalar_tensor_tensor(
                    out=ub.ap()[:, dc * TC:(dc + 1) * TC],
                    in0=xpb.ap()[:, dc * 260 + kk: dc * 260 + kk + TC],
                    scalar=convw_s.ap()[:, dc * 4 + kk: dc * 4 + kk + 1],
                    in1=ub.ap()[:, dc * TC:(dc + 1) * TC],
                    op0=ALU.mult, op1=ALU.add))
        I_("vector", sv, lambda q: q.tensor_copy(out=xpb3[:, :, 0:3], in_=xpb3[:, :, 256:259]))
        conv_done = sv.n
        W_("scalar", sv, conv_done)
        for dc in range(16):
            X_("scalar", lambda q, dc=dc: q.activation(
                out=ub.ap()[:, dc * TC:(dc + 1) * TC], in_=ub.ap()[:, dc * TC:(dc + 1) * TC],
                func=AF.Silu, bias=convb_s.ap()[:, dc:dc + 1]))
        N_("scalar", ss)
        u_done = ss.n

        if stage < 6:
            continue
        W_("tensor", ss, u_done)
        for k in range(16):
            X_("tensor", lambda q, k=k: q.matmul(
                pb[0].ap()[0:64, 0:TC], wxT_s.ap()[:, k * 96: k * 96 + 64],
                ub.ap()[:, k * TC:(k + 1) * TC], start=(k == 0), stop=(k == 15)))
        for k in range(16):
            X_("tensor", lambda q, k=k: q.matmul(
                pb[1].ap()[0:32, 0:TC], wxT_s.ap()[:, k * 96 + 64: k * 96 + 96],
                ub.ap()[:, k * TC:(k + 1) * TC], start=(k == 0), stop=(k == 15)))
        N_("tensor", st)
        W_("vector", st)
        X_("vector", lambda q: q.tensor_copy(out=xdbl.ap(), in_=pb[0].ap()[0:64, 0:TC]))
        I_("vector", sv, lambda q: q.tensor_copy(out=bcb.ap(), in_=pb[1].ap()[0:32, 0:TC]))
        xdbl_done = sv.n

        if stage < 7:
            continue
        W_("tensor", sv, xdbl_done)
        for m in range(16):
            bank = pb[2 + (m % 2)]
            if m >= 2:
                W_("tensor", ss)
            X_("tensor", lambda q, m=m, bank=bank: q.matmul(
                bank.ap()[:, 0:TC], wdtT_s.ap()[:, m * 128:(m + 1) * 128],
                xdbl.ap(), start=True, stop=True))
            N_("tensor", st)
            W_("scalar", st)
            X_("scalar", lambda q, m=m, bank=bank: q.activation(
                out=deltab.ap()[:, m * TC:(m + 1) * TC], in_=bank.ap()[:, 0:TC],
                func=AF.Exp, bias=bdt_s.ap()[:, m:m + 1]))
            I_("scalar", ss, lambda q, m=m: q.activation(
                out=deltab.ap()[:, m * TC:(m + 1) * TC],
                in_=deltab.ap()[:, m * TC:(m + 1) * TC], func=AF.Ln, bias=1.0))
        delta_done = ss.n

        if stage < 8:
            continue
        W_("vector", ss, delta_done)
        for dc in range(16):
            X_("vector", lambda q, dc=dc: q.tensor_mul(
                out=dub.ap()[:, dc * TC:(dc + 1) * TC],
                in0=deltab.ap()[:, dc * TC:(dc + 1) * TC],
                in1=ub.ap()[:, dc * TC:(dc + 1) * TC]))
        N_("vector", sv)
        du_done = sv.n

        pend_carry = None
        group_inc = {}
        gidx = 0
        for n in range(NST):
            W_("tensor", sv, du_done)
            if n >= 1:
                W_("tensor", sv)
            X_("tensor", lambda q, n=n: q.matmul(
                pb[4].ap()[:, 0:TC], sel_s.ap()[:, n * 128:(n + 1) * 128], bcb.ap(),
                start=True, stop=True))
            if own:
                X_("tensor", lambda q, n=n: q.matmul(
                    pb[5].ap()[:, 0:TC], sel_s.ap()[:, (16 + n) * 128:(17 + n) * 128], bcb.ap(),
                    start=True, stop=True))
            N_("tensor", st)
            W_("vector", st)
            X_("vector", lambda q: q.tensor_copy(out=Bb.ap(), in_=pb[4].ap()[:, 0:TC]))
            if own:
                X_("vector", lambda q: q.tensor_copy(out=Cb.ap(), in_=pb[5].ap()[:, 0:TC]))
            X_("vector", lambda q: q.tensor_copy(out=spacebuf.ap(), in_=carry.ap()[:, 0:4]))
            N_("vector", sv)
            bc_ready = sv.n

            for dc in range(16):
                slot = dc % 2
                W_("scalar", sv, dA_free[slot])
                col = dc * 16 + n
                X_("scalar", lambda q, dc=dc, col=col, slot=slot: q.activation(
                    out=dAb.ap()[:, slot * TC:(slot + 1) * TC],
                    in_=deltab.ap()[:, dc * TC:(dc + 1) * TC], func=AF.Exp,
                    scale=abuf.ap()[:, col:col + 1]))
                N_("scalar", ss)
                dA_ready = ss.n
                rs_ = gidx % 4
                if gidx - 4 in group_inc:
                    W_("gpsimd", sv, group_inc[gidx - 4])
                W_("gpsimd", sv, bc_ready)
                I_("gpsimd", sp, lambda q, dc=dc, rs_=rs_: q.tensor_tensor(
                    out=dBub.ap()[:, rs_ * TC:(rs_ + 1) * TC],
                    in0=dub.ap()[:, dc * TC:(dc + 1) * TC], in1=Bb.ap(), op=ALU.mult))
                dbu_ready = sp.n
                W_("vector", ss, dA_ready)
                W_("vector", sp, dbu_ready)
                sslot = (n * 16 + dc) % 2
                X_("vector", lambda q, slot=slot, col=col, sslot=sslot, rs_=rs_: q.tensor_tensor_scan(
                    out=sst.ap()[:, sslot * TC:(sslot + 1) * TC],
                    data0=dAb.ap()[:, slot * TC:(slot + 1) * TC],
                    data1=dBub.ap()[:, rs_ * TC:(rs_ + 1) * TC],
                    initial=carry.ap()[:, col:col + 1],
                    op0=ALU.mult, op1=ALU.add))
                if pend_carry is not None:
                    pc, ps_ = pend_carry
                    X_("vector", lambda q, pc=pc, ps_=ps_: q.tensor_copy(
                        out=carry.ap()[:, pc:pc + 1],
                        in_=sst.ap()[:, ps_ * TC + TC - 1: ps_ * TC + TC]))
                pend_carry = (col, sslot)
                if own:
                    if n == 0:
                        X_("vector", lambda q, dc=dc, sslot=sslot: q.tensor_mul(
                            out=yac.ap()[:, dc * TC:(dc + 1) * TC],
                            in0=sst.ap()[:, sslot * TC:(sslot + 1) * TC], in1=Cb.ap()))
                    else:
                        X_("vector", lambda q, sslot=sslot: q.tensor_mul(
                            out=tmpv.ap(), in0=sst.ap()[:, sslot * TC:(sslot + 1) * TC],
                            in1=Cb.ap()))
                        X_("vector", lambda q, dc=dc: q.tensor_add(
                            out=yac.ap()[:, dc * TC:(dc + 1) * TC],
                            in0=yac.ap()[:, dc * TC:(dc + 1) * TC], in1=tmpv.ap()))
                N_("vector", sv)
                dA_free[slot] = sv.n
                group_inc[gidx] = sv.n
                gidx += 1

        if pend_carry is not None:
            X_("vector", lambda q: q.tensor_copy(out=spacebuf.ap(), in_=carry.ap()[:, 0:4]))
            pc, ps_ = pend_carry
            I_("vector", sv, lambda q, pc=pc, ps_=ps_: q.tensor_copy(
                out=carry.ap()[:, pc:pc + 1],
                in_=sst.ap()[:, ps_ * TC + TC - 1: ps_ * TC + TC]))

        if not own or stage < 9:
            continue

        for dc in range(16):
            X_("vector", lambda q, dc=dc: q.scalar_tensor_tensor(
                out=tmpv.ap(), in0=ub.ap()[:, dc * TC:(dc + 1) * TC],
                scalar=dprm_s.ap()[:, dc:dc + 1],
                in1=yac.ap()[:, dc * TC:(dc + 1) * TC], op0=ALU.mult, op1=ALU.add))
            X_("vector", lambda q, dc=dc: q.tensor_mul(
                out=yac.ap()[:, dc * TC:(dc + 1) * TC], in0=tmpv.ap(),
                in1=szb.ap()[:, dc * TC:(dc + 1) * TC]))
        N_("vector", sv)
        y_done = sv.n
        prev_y_done = y_done

        W_("tensor", sv, y_done)
        for k in range(16):
            slot = k % 4
            W_("sync", st, wbuf_free[slot])
            vw = dma("sync", sq, wbuf.ap()[:, slot * 1024:(slot + 1) * 1024],
                     W_outT[k * 128:(k + 1) * 128, :])
            W_("tensor", sq, vw)
            for ts in range(2):
                for nn in range(2):
                    X_("tensor", lambda q, k=k, ts=ts, nn=nn, slot=slot: q.matmul(
                        pb[ts * 2 + nn].ap(),
                        yac.ap()[:, k * TC + ts * 128: k * TC + (ts + 1) * 128],
                        wbuf.ap()[:, slot * 1024 + nn * 512: slot * 1024 + (nn + 1) * 512],
                        start=(k == 0), stop=(k == 15)))
            N_("tensor", st)
            wbuf_free[slot] = st.n
        wout_done = st.n
        W_("vector", st, wout_done)
        W_("vector", sg, x1_spill)
        for ts in range(2):
            for nn in range(2):
                X_("vector", lambda q, ts=ts, nn=nn: q.tensor_add(
                    out=x1tm.ap()[:, ts * DM + nn * 512: ts * DM + (nn + 1) * 512],
                    in0=xtm.ap()[:, ts * DM + nn * 512: ts * DM + (nn + 1) * 512],
                    in1=pb[ts * 2 + nn].ap()))
        N_("vector", sv)
        x1_done = sv.n
        xtm_free = x1_done
        W_("gpsimd", sv, x1_done)
        for j in range(2):
            x1_spill = dma("gpsimd", sg, x1_d[t0 + j * 128: t0 + (j + 1) * 128, :],
                           x1tm.ap()[:, j * DM:(j + 1) * DM])

        if stage < 10:
            continue
        W_("scalar", sv, x1_done)
        for j in range(2):
            X_("scalar", lambda q, j=j: q.activation(
                out=scr_tm.ap()[:, j * DM:(j + 1) * DM],
                in_=x1tm.ap()[:, j * DM:(j + 1) * DM], func=AF.Identity,
                accum_out=sums.ap()[:, j:j + 1]))
        N_("scalar", ss)
        W_("vector", ss)
        I_("vector", sv, lambda q: q.tensor_scalar_mul(out=nmean.ap(), in0=sums.ap(),
                                                       scalar1=-1.0 / DM))
        W_("scalar", sv)
        for j in range(2):
            X_("scalar", lambda q, j=j: q.activation(
                out=scr_tm.ap()[:, j * DM:(j + 1) * DM],
                in_=x1tm.ap()[:, j * DM:(j + 1) * DM], func=AF.Square,
                bias=nmean.ap()[:, j:j + 1], accum_out=ssq.ap()[:, j:j + 1]))
        W_("scalar", sv, cst_ready)
        I_("scalar", ss, lambda q: q.activation(out=sdv.ap(), in_=ssq.ap(), func=AF.Sqrt,
                                                scale=cst.ap()[:, 1:2], bias=cst.ap()[:, 0:1]))
        W_("vector", ss)
        X_("vector", lambda q: q.reciprocal(out=rsv.ap(), in_=sdv.ap()))
        X_("vector", lambda q: q.tensor_copy(out=spacebuf.ap()[:, 0:2], in_=sdv.ap()))
        for j in range(2):
            X_("vector", lambda q, j=j: q.tensor_scalar(
                out=scr_tm.ap()[:, j * DM:(j + 1) * DM], in0=x1tm.ap()[:, j * DM:(j + 1) * DM],
                scalar1=nmean.ap()[:, j:j + 1], scalar2=rsv.ap()[:, j:j + 1],
                op0=ALU.add, op1=ALU.mult))
        N_("vector", sv)
        xh2_done = sv.n
        W_("tensor", sv, xh2_done)
        for j in range(2):
            for b_ in range(8):
                bank = pb[(j * 8 + b_) % 2]
                if j * 8 + b_ >= 2:
                    W_("tensor", sv)
                I_("tensor", st, lambda q, j=j, b_=b_, bank=bank: q.transpose(
                    bank.ap()[:, 0:128], scr_tm.ap()[:, j * DM + b_ * 128: j * DM + (b_ + 1) * 128],
                    ident_s.ap()))
                W_("vector", st)
                I_("vector", sv, lambda q, j=j, b_=b_, bank=bank: q.tensor_copy(
                    out=lnx_f.ap()[:, b_ * TC + j * 128: b_ * TC + (j + 1) * 128],
                    in_=bank.ap()[:, 0:128]))
        xhf_done = sv.n
        W_("gpsimd", sv, xhf_done)
        for dc in range(8):
            xhat_spill = dma("gpsimd", sg, xhat_d[dc * 128:(dc + 1) * 128, t0: t0 + TC],
                             lnx_f.ap()[:, dc * TC:(dc + 1) * TC])

        if stage < 11:
            continue
        for dc in range(8):
            X_("vector", lambda q, dc=dc: q.tensor_scalar(
                out=scr_tm.ap()[:, dc * TC:(dc + 1) * TC],
                in0=lnx_f.ap()[:, dc * TC:(dc + 1) * TC],
                scalar1=rng_s.ap()[:, dc:dc + 1], scalar2=rnb_s.ap()[:, dc:dc + 1],
                op0=ALU.mult, op1=ALU.add))
        N_("vector", sv)
        normed_done = sv.n
        W_("tensor", sv, normed_done)
        for k in range(8):
            X_("tensor", lambda q, k=k: q.matmul(
                pb[6].ap()[0:8, 0:TC], rwT_s.ap()[:, k * 8:(k + 1) * 8],
                scr_tm.ap()[:, k * TC:(k + 1) * TC], start=(k == 0), stop=(k == 7)))
        N_("tensor", st)
        W_("vector", st)
        I_("vector", sv, lambda q: q.tensor_scalar_add(
            out=lt_f.ap(), in0=pb[6].ap()[0:8, 0:TC], scalar1=rb_s.ap()))
        logits_done = sv.n
        if dbg and ch == nch_pre:
            W_("gpsimd", sv, logits_done)
            dma("gpsimd", sg, lt_dbg, lt_f.ap())
        W_("tensor", sv, logits_done)
        for j in range(2):
            X_("tensor", lambda q, j=j: q.transpose(
                pb[7].ap()[:, j * 8:(j + 1) * 8], lt_f.ap()[:, j * 128:(j + 1) * 128],
                ident_s.ap()[0:8, 0:8]))
        N_("tensor", st)
        W_("vector", st)
        for j in range(2):
            X_("vector", lambda q, j=j: q.tensor_copy(
                out=lt_tm.ap()[:, j * 8:(j + 1) * 8], in_=pb[7].ap()[:, j * 8:(j + 1) * 8]))
        for j in range(2):
            X_("vector", lambda q, j=j: q.reduce_max(
                out=m1c.ap()[:, j:j + 1], in_=lt_tm.ap()[:, j * 8:(j + 1) * 8], axis=AX.X))
        for j in range(2):
            X_("vector", lambda q, j=j: q.tensor_single_scalar(
                out=ge1.ap()[:, j * 8:(j + 1) * 8], in_=lt_tm.ap()[:, j * 8:(j + 1) * 8],
                scalar=m1c.ap()[:, j:j + 1], op=ALU.is_ge))
        for j in range(2):
            X_("vector", lambda q, j=j: q.scalar_tensor_tensor(
                out=ge2.ap()[:, j * 8:(j + 1) * 8], in0=ge1.ap()[:, j * 8:(j + 1) * 8],
                scalar=-1e30, in1=lt_tm.ap()[:, j * 8:(j + 1) * 8],
                op0=ALU.mult, op1=ALU.add))
        for j in range(2):
            X_("vector", lambda q, j=j: q.reduce_max(
                out=m2c.ap()[:, j:j + 1], in_=ge2.ap()[:, j * 8:(j + 1) * 8], axis=AX.X))
        for j in range(2):
            X_("vector", lambda q, j=j: q.tensor_single_scalar(
                out=ge2.ap()[:, j * 8:(j + 1) * 8], in_=ge2.ap()[:, j * 8:(j + 1) * 8],
                scalar=m2c.ap()[:, j:j + 1], op=ALU.is_ge))
        X_("vector", lambda q: q.tensor_sub(out=dcol.ap(), in0=m1c.ap(), in1=m2c.ap()))
        N_("vector", sv)
        if dbg and ch == nch_pre:
            X_("vector", lambda q: q.tensor_copy(out=dbg_s.ap()[:, 0:8], in_=lt_tm.ap()[:, 0:8]))
            X_("vector", lambda q: q.tensor_copy(out=dbg_s.ap()[:, 8:9], in_=m1c.ap()[:, 0:1]))
            I_("vector", sv, lambda q: q.tensor_copy(out=dbg_s.ap()[:, 9:10], in_=m2c.ap()[:, 0:1]))
            W_("gpsimd", sv)
            dma("gpsimd", sg, t8_dbg, dbg_s.ap())
        W_("scalar", sv)
        X_("scalar", lambda q: q.activation(out=w1c.ap(), in_=dcol.ap(), func=AF.Sigmoid))
        X_("scalar", lambda q: q.activation(out=w2c.ap(), in_=dcol.ap(), func=AF.Sigmoid,
                                            scale=-1.0))
        N_("scalar", ss)
        W_("vector", ss)
        for j in range(2):
            tg = (ch - nch_pre) * 2 + j
            X_("vector", lambda q, tg=tg, j=j: q.tensor_scalar_mul(
                out=comb_s.ap()[:, tg * 8:(tg + 1) * 8], in0=ge1.ap()[:, j * 8:(j + 1) * 8],
                scalar1=w1c.ap()[:, j:j + 1]))
        for j in range(2):
            tg = (ch - nch_pre) * 2 + j
            X_("vector", lambda q, tg=tg, j=j: q.scalar_tensor_tensor(
                out=comb_s.ap()[:, tg * 8:(tg + 1) * 8], in0=ge2.ap()[:, j * 8:(j + 1) * 8],
                scalar=w2c.ap()[:, j:j + 1],
                in1=comb_s.ap()[:, tg * 8:(tg + 1) * 8], op0=ALU.mult, op1=ALU.add))
        N_("vector", sv)

    if dbg:
        W_("gpsimd", sv)
        dma("gpsimd", sg, comb_dbg, comb_s.ap())
    bar_t, bar_v, bar_s = st.n, sv.n, ss.n
    bar_q, bar_g = sq.n, sg.n
    for eng in ENGS:
        W_(eng, st, bar_t)
        W_(eng, sv, bar_v)
        W_(eng, ss, bar_s)
        W_(eng, sq, bar_q)
        W_(eng, sg, bar_g)

    _f = _Bump()

    def fsb(name, shape, dtype=F32):
        assert dtype is F32
        return _f.take(shape)

    xh_f = fsb("xh_f", (128, 8 * FTC))
    h_f = fsb("h_f", (128, 8 * FTC))
    g_f = fsb("g_f", (128, 32 * FTC))
    x1f = fsb("x1f", (128, 4 * 1024))
    wbuf1 = fsb("wbuf1", (128, 4 * 512))
    wbuf2 = fsb("wbuf2", (128, 4 * 1024))
    tmpf = fsb("tmpf", (128, 512))

    w1_free = [0, 0, 0, 0]
    w2_free = [0, 0, 0, 0]
    evac_done = [0]
    st_out_total = 0
    xh_free = 0
    x1f_free = 0

    for fc in range(nfc if stage >= 12 else 0):
        ft0 = fc * FTC
        W_("sync", sv, xh_free)
        W_("sync", sg, x1f_free)
        vxh = 0
        for dc in range(8):
            vxh = dma("sync", sq, xh_f.ap()[:, dc * FTC:(dc + 1) * FTC],
                      xhat_d[dc * 128:(dc + 1) * 128, ft0: ft0 + FTC])
        vx1 = 0
        for ts in range(4):
            vx1 = dma("sync", sq, x1f.ap()[:, ts * 1024:(ts + 1) * 1024],
                      x1_d[ft0 + ts * 128: ft0 + (ts + 1) * 128, :])
        W_("vector", sq, vx1)

        for e in range(nexp):
            if e == 0:
                W_("vector", sq, vxh)
                W_("vector", st)
                for dc in range(8):
                    X_("vector", lambda q, dc=dc: q.tensor_scalar(
                        out=h_f.ap()[:, dc * FTC:(dc + 1) * FTC],
                        in0=xh_f.ap()[:, dc * FTC:(dc + 1) * FTC],
                        scalar1=lng_s.ap()[:, dc: dc + 1],
                        scalar2=lnb_s.ap()[:, dc: dc + 1],
                        op0=ALU.mult, op1=ALU.add))
                N_("vector", sv)
                hf_done = sv.n
            else:
                hf_done = hf_next_done

            if stage < 13:
                continue
            W_("tensor", sv, hf_done)
            W_("tensor", sv, evac_done[0])
            for mgh in range(8):
                bb = (mgh % 2) * 4
                if mgh > 1:
                    W_("tensor", ss)
                for k in range(8):
                    slot = (mgh * 8 + k) % 4
                    W_("sync", st, w1_free[slot])
                    vw = dma("sync", sq, wbuf1.ap()[:, slot * 512:(slot + 1) * 512],
                             w1[e, k * 128:(k + 1) * 128, mgh * 512:(mgh + 1) * 512])
                    W_("tensor", sq, vw)
                    for m in range(4):
                        X_("tensor", lambda q, slot=slot, m=m, k=k, bb=bb: q.matmul(
                            pb[bb + m].ap(),
                            wbuf1.ap()[:, slot * 512 + m * 128: slot * 512 + (m + 1) * 128],
                            h_f.ap()[:, k * FTC:(k + 1) * FTC],
                            start=(k == 0), stop=(k == 7)))
                    N_("tensor", st)
                    w1_free[slot] = st.n
                mg_done = st.n
                W_("scalar", st, mg_done)
                for m in range(4):
                    ic = mgh * 4 + m
                    X_("scalar", lambda q, e=e, m=m, ic=ic, bb=bb: q.activation(
                        out=g_f.ap()[:, ic * FTC:(ic + 1) * FTC], in_=pb[bb + m].ap(),
                        func=AF.Gelu, bias=b1_s.ap()[:, e * 32 + ic: e * 32 + ic + 1]))
                N_("scalar", ss)
            g_done = ss.n
            w1_all_done = st.n
            if dbg and fc == 0 and e == 0:
                W_("gpsimd", ss, g_done)
                dma("gpsimd", sg, g_dbg, g_f.ap())
                W_("gpsimd", sv, hf_done)
                dma("gpsimd", sg, hf_dbg, h_f.ap())

            if e + 1 < nexp:
                W_("vector", st, w1_all_done)
                for dc in range(8):
                    X_("vector", lambda q, e=e, dc=dc: q.tensor_scalar(
                        out=h_f.ap()[:, dc * FTC:(dc + 1) * FTC],
                        in0=xh_f.ap()[:, dc * FTC:(dc + 1) * FTC],
                        scalar1=lng_s.ap()[:, (e + 1) * 8 + dc: (e + 1) * 8 + dc + 1],
                        scalar2=lnb_s.ap()[:, (e + 1) * 8 + dc: (e + 1) * 8 + dc + 1],
                        op0=ALU.mult, op1=ALU.add))
                N_("vector", sv)
                hf_next_done = sv.n

            if stage < 14:
                continue
            W_("tensor", ss, g_done)
            if e > 0 or fc > 0:
                W_("vector", sv)
            for ts in range(4):
                for nn in range(2):
                    X_("tensor", lambda q, e=e, ts=ts, nn=nn: q.matmul(
                        pb[ts * 2 + nn].ap(), sel_s.ap()[0:8, e * 128:(e + 1) * 128],
                        b2_s.ap()[:, nn * 512:(nn + 1) * 512],
                        start=True, stop=False))
            for k2 in range(32):
                slot = k2 % 4
                W_("sync", st, w2_free[slot])
                vw = dma("sync", sq, wbuf2.ap()[:, slot * 1024:(slot + 1) * 1024],
                         w2[e, k2 * 128:(k2 + 1) * 128, :])
                W_("tensor", sq, vw)
                for ts in range(4):
                    for nn in range(2):
                        X_("tensor", lambda q, k2=k2, ts=ts, nn=nn, slot=slot: q.matmul(
                            pb[ts * 2 + nn].ap(),
                            g_f.ap()[:, k2 * FTC + ts * 128: k2 * FTC + (ts + 1) * 128],
                            wbuf2.ap()[:, slot * 1024 + nn * 512: slot * 1024 + (nn + 1) * 512],
                            start=False, stop=(k2 == 31)))
                N_("tensor", st)
                w2_free[slot] = st.n
            w2_done = st.n
            W_("vector", st, w2_done)
            for ts in range(4):
                tg = fc * 4 + ts
                for nn in range(2):
                    X_("vector", lambda q, e=e, ts=ts, nn=nn, tg=tg: q.tensor_scalar_mul(
                        out=tmpf.ap(), in0=pb[ts * 2 + nn].ap(),
                        scalar1=comb_s.ap()[:, tg * 8 + e: tg * 8 + e + 1]))
                    X_("vector", lambda q, ts=ts, nn=nn: q.tensor_add(
                        out=x1f.ap()[:, ts * 1024 + nn * 512: ts * 1024 + (nn + 1) * 512],
                        in0=x1f.ap()[:, ts * 1024 + nn * 512: ts * 1024 + (nn + 1) * 512],
                        in1=tmpf.ap()))
            N_("vector", sv)
            evac_done[0] = sv.n

        out_ready = sv.n
        xh_free = out_ready
        W_("gpsimd", sv, out_ready)
        for ts in range(4):
            st_out_total = dma("gpsimd", sg, out_d[ft0 + ts * 128: ft0 + (ts + 1) * 128, :],
                               x1f.ap()[:, ts * 1024:(ts + 1) * 1024])
        x1f_free = st_out_total

    if st_out_total == 0:
        for ts in range(4):
            st_out_total = dma("gpsimd", sg, out_d[ts * 128:(ts + 1) * 128, :],
                               arena.ap()[:, 0:1024])
    W_("gpsimd", sg, st_out_total)

    with nc.Block() as block:
        @block.sync
        def _(q):
            for f in ops["sync"]:
                f(q)

        @block.gpsimd
        def _(q):
            for f in ops["gpsimd"]:
                f(q)

        @block.tensor
        def _(q):
            for f in ops["tensor"]:
                f(q)

        @block.vector
        def _(q):
            for f in ops["vector"]:
                f(q)

        @block.scalar
        def _(q):
            for f in ops["scalar"]:
                f(q)

    nc.reset()
    stack.pop_all()
    return nc


def _host_layout(inputs):
    g = lambda k: np.ascontiguousarray(np.asarray(inputs[k], dtype=np.float32))
    x = g("L0_attn_x")
    W_in = g("L0_attn_W_in")
    conv_w = g("L0_attn_conv_w")
    conv_b = g("L0_attn_conv_b")
    W_x = g("L0_attn_W_x")
    W_dt = g("L0_attn_W_dt")
    b_dt = g("L0_attn_b_dt")
    A_log = g("L0_attn_A_log")
    D_param = g("L0_attn_D_param")
    W_out = g("L0_attn_W_out")
    rn_g = g("L0_ffn_rn_g")
    rn_b = g("L0_ffn_rn_b")
    router_w = g("L0_ffn_router_w")
    router_b = g("L0_ffn_router_b")
    ln_g = g("L0_ffn_ln_g")
    ln_b = g("L0_ffn_ln_b")
    w1 = g("L0_ffn_w1")
    b1 = g("L0_ffn_b1")
    w2 = g("L0_ffn_w2")
    b2 = g("L0_ffn_b2")

    C = np.ascontiguousarray
    shared = {
        "W_inT": C(W_in.T),
        "W_xT": C(W_x.T),
        "W_dtT": C(W_dt.T),
        "W_outT": C(W_out.T),
        "convw": C(conv_w[:, 0, :].reshape(16, 128, 4).transpose(1, 0, 2).reshape(128, 64)),
        "convb": C(conv_b.reshape(16, 128).T),
        "bdt": C(b_dt.reshape(16, 128).T),
        "alog": C(A_log.reshape(16, 128, 16).transpose(1, 0, 2).reshape(128, 256)),
        "dprm": C(D_param.reshape(16, 128).T),
        "rng_g": C(rn_g.reshape(8, 128).T),
        "rnb_g": C(rn_b.reshape(8, 128).T),
        "rwT": C(router_w.T),
        "rbias": C(router_b.reshape(8, 1)),
        "lng": C(ln_g.reshape(8, 8, 128).transpose(2, 0, 1).reshape(128, 64)),
        "lnb": C(ln_b.reshape(8, 8, 128).transpose(2, 0, 1).reshape(128, 64)),
        "w1": w1,
        "b1l": C(b1.reshape(8, 32, 128).transpose(2, 0, 1).reshape(128, 256)),
        "w2": w2,
        "b2m": b2,
        "ident": np.eye(128, dtype=np.float32),
        "sel": C(np.kron(np.eye(32, dtype=np.float32), np.ones((1, 128), np.float32))),
    }
    in_maps = []
    for c in range(8):
        b, h = c // 2, c % 2
        m = dict(shared)
        m["x_pre"] = C(x[b, 0:TOWN, :])
        m["x_own"] = C(x[b, h * TOWN:(h + 1) * TOWN, :])
        m["flag"] = np.full((128, 1), float(h), dtype=np.float32)
        in_maps.append(m)
    return in_maps


def kernel(**inputs):
    in_maps = _host_layout(inputs)
    nc = build_nc()
    res = run_bass_kernel_spmd(nc, in_maps, core_ids=list(range(8)))
    out = np.empty((B, L, DM), dtype=np.float32)
    for c in range(8):
        b, h = c // 2, c % 2
        out[b, h * TOWN:(h + 1) * TOWN, :] = res.results[c]["out"]
    return out


# version 07
# speedup vs baseline: 1.3003x; 1.2985x over previous
import numpy as np
import concourse.bass as bass
import concourse.mybir as mybir
from concourse.bass_utils import run_bass_kernel_spmd

F32 = mybir.dt.float32
AF = mybir.ActivationFunctionType
ALU = mybir.AluOpType
AX = mybir.AxisListType

B, L, DM = 4, 4096, 1024
DI = 2048
NST = 16
DTR = 64
E, II = 8, 4096
TOWN = 2048
TC = 256
NCH_PRE, NCH_OWN = 8, 8
FTC = 512
NFC = TOWN // FTC
EPS = 1e-5

ENGS = ("sync", "gpsimd", "tensor", "vector", "scalar")


class Sem:
    def __init__(self, h):
        self.h = h
        self.n = 0


def build_nc(nch_pre=NCH_PRE, nch_own=NCH_OWN, nfc=NFC, stage=99, dbg=False, nexp=E):
    import contextlib
    nc = bass.Bass("TRN2", target_bir_lowering=False, detect_race_conditions=False)
    stack = contextlib.ExitStack()

    d_in = {}
    def din(name, shape):
        d_in[name] = nc.dram_tensor(name, list(shape), F32, kind="ExternalInput").ap()
        return d_in[name]

    x_pre = din("x_pre", (TOWN, DM))
    x_own = din("x_own", (TOWN, DM))
    flag = din("flag", (128, 1))
    W_inT = din("W_inT", (DM, 2 * DI))
    W_xT = din("W_xT", (DI, 96))
    W_dtT = din("W_dtT", (DTR, DI))
    W_outT = din("W_outT", (DI, DM))
    convw = din("convw", (128, 64))
    convb = din("convb", (128, 16))
    bdt = din("bdt", (128, 16))
    alog = din("alog", (128, 256))
    dprm = din("dprm", (128, 16))
    rng_g = din("rng_g", (128, 8))
    rnb_g = din("rnb_g", (128, 8))
    rwT = din("rwT", (DM, E))
    rbias = din("rbias", (E, 1))
    lng = din("lng", (128, 64))
    lnb = din("lnb", (128, 64))
    w1 = din("w1", (E, DM, II))
    b1l = din("b1l", (128, 256))
    w2 = din("w2", (E, II, DM))
    b2m = din("b2m", (E, DM))
    ident = din("ident", (128, 128))
    sel = din("sel", (32, 32 * 128))
    iota = din("iota", (128, 4))

    out_d = nc.dram_tensor("out", [TOWN, DM], F32, kind="ExternalOutput").ap()
    _ik = "ExternalOutput" if dbg else "Internal"
    g_dbg = nc.dram_tensor("g_dbg", [128, 32 * FTC], F32, kind=_ik).ap()
    comb_dbg = nc.dram_tensor("comb_dbg", [128, 16 * 8], F32, kind=_ik).ap()
    lt_dbg = nc.dram_tensor("lt_dbg", [8, 256], F32, kind=_ik).ap()
    t8_dbg = nc.dram_tensor("t8_dbg", [128, 16], F32, kind=_ik).ap()
    hf_dbg = nc.dram_tensor("hf_dbg", [128, 8 * FTC], F32, kind=_ik).ap()
    xhtm_d = nc.dram_tensor("xhtm_d", [TOWN, DM], F32, kind=_ik).ap()
    x1_d = nc.dram_tensor("x1_d", [TOWN, DM], F32, kind=_ik).ap()

    sq = Sem(stack.enter_context(nc.semaphore("sq")))
    sp = Sem(stack.enter_context(nc.semaphore("sp")))
    sg = Sem(stack.enter_context(nc.semaphore("sg")))
    st = Sem(stack.enter_context(nc.semaphore("st")))
    sv = Sem(stack.enter_context(nc.semaphore("sv")))
    ss = Sem(stack.enter_context(nc.semaphore("ss")))

    ops = {e: [] for e in ENGS}

    def E_(eng, fn):
        ops[eng].append(fn)

    def W_(eng, sem, val=None):
        v = sem.n if val is None else val
        if v <= 0:
            return
        E_(eng, lambda q, h=sem.h, vv=v: q.wait_ge(h, vv))

    def I_(eng, sem, fn, k=1):
        sem.n += k
        E_(eng, lambda q, f=fn, h=sem.h, kk=k: f(q).then_inc(h, kk))

    def X_(eng, fn):
        E_(eng, fn)

    def N_(eng, sem, k=1):
        sem.n += k
        last = ops[eng].pop()
        ops[eng].append(lambda q, f=last, h=sem.h, kk=k: f(q).then_inc(h, kk))

    def dma(queue, sem, out_ap, in_ap):
        I_(queue, sem, lambda q, o=out_ap, i=in_ap: q.dma_start(out=o, in_=i), 16)
        return sem.n

    def sb(name, shape, dtype=F32):
        return stack.enter_context(nc.sbuf_tensor(name, list(shape), dtype))

    ident_s = sb("ident_s", (128, 128))
    sel_s = sb("sel_s", (32, 32 * 128))
    wxT_s = sb("wxT_s", (128, 16 * 96))
    wdtT_s = sb("wdtT_s", (64, DI))
    rwT_s = sb("rwT_s", (128, 64))
    abuf = sb("abuf", (128, 256))
    convw_s = sb("convw_s", (128, 64))
    convb_s = sb("convb_s", (128, 16))
    bdt_s = sb("bdt_s", (128, 16))
    dprm_s = sb("dprm_s", (128, 16))
    rng_s = sb("rng_s", (128, 8))
    rnb_s = sb("rnb_s", (128, 8))
    lng_s = sb("lng_s", (128, 64))
    lnb_s = sb("lnb_s", (128, 64))
    b1_s = sb("b1_s", (128, 256))
    b2_s = sb("b2_s", (E, DM))
    rb_s = sb("rb_s", (8, 1))
    flag_s = sb("flag_s", (128, 1))
    ones1 = sb("ones1", (1, 128))
    cst = sb("cst", (128, 2))
    comb_s = sb("comb_s", (128, 16 * 8))
    comb_f = sb("comb_f", (8, TOWN))
    iota_s = sb("iota_s", (128, 4))
    dbg_s = sb("dbg_s", (128, 16))

    pb = [stack.enter_context(nc.psum_tensor(f"pb{i}", [128, 512], F32)) for i in range(8)]

    for dst, src in [(ident_s, ident), (sel_s, sel), (wdtT_s, W_dtT),
                     (convw_s, convw), (convb_s, convb), (bdt_s, bdt),
                     (dprm_s, dprm), (rng_s, rng_g), (rnb_s, rnb_g),
                     (lng_s, lng), (lnb_s, lnb), (b1_s, b1l), (b2_s, b2m),
                     (rb_s, rbias), (flag_s, flag), (iota_s, iota)]:
        dma("sync", sq, dst.ap(), src)
    for k in range(16):
        dma("sync", sq, wxT_s.ap()[:, k * 96:(k + 1) * 96], W_xT[k * 128:(k + 1) * 128, :])
    for k in range(8):
        dma("sync", sq, rwT_s.ap()[:, k * 8:(k + 1) * 8], rwT[k * 128:(k + 1) * 128, :])
    v_alog = dma("sync", sq, abuf.ap(), alog)
    consts_loaded = sq.n
    W_("scalar", sq, v_alog)
    I_("scalar", ss, lambda q: q.activation(out=abuf.ap(), in_=abuf.ap(), func=AF.Exp))
    W_("vector", ss)
    I_("vector", sv, lambda q: q.tensor_scalar_mul(out=abuf.ap(), in0=abuf.ap(), scalar1=-1.0))
    X_("vector", lambda q: q.memset(ones1.ap(), 1.0))
    X_("vector", lambda q: q.memset(cst.ap()[:, 0:1], EPS))
    I_("vector", sv, lambda q: q.memset(cst.ap()[:, 1:2], 1.0 / DM))
    abuf_ready = sv.n
    cst_ready = sv.n

    ARENA_W = 41184
    arena = sb("arena", (128, ARENA_W))

    class _Buf:
        __slots__ = ("_ap",)
        def __init__(self, ap_):
            self._ap = ap_
        def ap(self):
            return self._ap

    class _Bump:
        def __init__(self):
            self.off = 0
        def take(self, shape):
            assert shape[0] <= 128
            w = int(np.prod(shape[1:]))
            w_al = (w + 15) // 16 * 16
            ap_ = arena.ap()[0:shape[0], self.off:self.off + w]
            self.off += w_al
            assert self.off <= ARENA_W, f"arena overflow {self.off}"
            return _Buf(ap_)

    _m = _Bump()

    def msb(name, shape, dtype=F32):
        assert dtype is F32
        return _m.take(shape)

    xtm = msb("xtm", (128, 2 * DM))
    scr_tm = msb("scr_tm", (128, 2 * DM))
    x1tm = msb("x1tm", (128, 2 * DM))
    lnx_f = msb("lnx_f", (128, 8 * TC))
    wbuf = msb("wbuf", (128, 4 * 1024))
    xpb = msb("xpb", (128, 16 * 260))
    szb = msb("szb", (128, 16 * TC))
    ub = msb("ub", (128, 16 * TC))
    deltab = msb("deltab", (128, 16 * TC))
    dub = msb("dub", (128, 16 * TC))
    yac = msb("yac", (128, 16 * TC))
    xdbl = msb("xdbl", (64, TC))
    bcb = msb("bcb", (32, TC))
    Bb = msb("Bb", (128, TC))
    Cb = msb("Cb", (128, TC))
    dAb = msb("dAb", (128, 2 * TC))
    dBub = msb("dBub", (128, 4 * TC))
    sst = msb("sst", (128, 2 * TC))
    spacebuf = msb("spacebuf", (128, 4))
    tmpv = msb("tmpv", (128, TC))
    carry = msb("carry", (128, 256))
    sums = msb("sums", (128, 2))
    nmean = msb("nmean", (128, 2))
    ssq = msb("ssq", (128, 2))
    sdv = msb("sdv", (128, 2))
    rsv = msb("rsv", (128, 2))
    lt_f = msb("lt_f", (8, TC))
    lt_tm = msb("lt_tm", (128, 16))
    top8 = msb("top8", (128, 8))
    m1c = msb("m1c", (128, 2))
    m2c = msb("m2c", (128, 2))
    dcol = msb("dcol", (128, 2))
    w1c = msb("w1c", (128, 2))
    w2c = msb("w2c", (128, 2))
    w12 = msb("w12", (128, 1))
    ge1 = msb("ge1", (128, 16))
    ge2 = msb("ge2", (128, 16))

    W_("vector", sq, consts_loaded)
    X_("vector", lambda q: q.memset(carry.ap(), 0.0))
    xpb3 = xpb.ap().rearrange("p (c w) -> p c w", w=260)
    X_("vector", lambda q: q.memset(xpb3[:, :, 0:3], 0.0))
    I_("vector", sv, lambda q: q.memset(yac.ap(), 0.0))

    wbuf_free = [0, 0, 0, 0]
    dA_free = [abuf_ready, abuf_ready]
    xtm_free = 0
    x1_spill = 0
    xhat_spill = 0
    prev_y_done = 0

    for ch in range(nch_pre + nch_own):
        own = ch >= nch_pre
        t0 = (ch - nch_pre) * TC if own else (ch * TC) % TOWN
        xsrc = x_own if own else x_pre

        if ch == nch_pre:
            W_("vector", sq, consts_loaded)
            X_("vector", lambda q: q.tensor_copy(out=spacebuf.ap(), in_=carry.ap()[:, 4:8]))
            X_("vector", lambda q: q.tensor_copy(out=spacebuf.ap(), in_=carry.ap()[:, 8:12]))
            X_("vector", lambda q: q.tensor_scalar_mul(out=carry.ap(), in0=carry.ap(),
                                                       scalar1=flag_s.ap()))
            tails = xpb3[:, :, 0:3]
            I_("vector", sv, lambda q, t=tails: q.tensor_scalar_mul(out=t, in0=t,
                                                                    scalar1=flag_s.ap()))

        if stage < 2:
            continue
        W_("sync", sv, xtm_free)
        vx = 0
        for j in range(2):
            vx = dma("sync", sq, xtm.ap()[:, j * DM:(j + 1) * DM],
                     xsrc[t0 + j * 128: t0 + (j + 1) * 128, :])

        W_("scalar", sq, vx)
        for j in range(2):
            X_("scalar", lambda q, j=j: q.activation(
                out=scr_tm.ap()[:, j * DM:(j + 1) * DM],
                in_=xtm.ap()[:, j * DM:(j + 1) * DM], func=AF.Identity,
                accum_out=sums.ap()[:, j:j + 1]))
        N_("scalar", ss)
        W_("vector", ss)
        I_("vector", sv, lambda q: q.tensor_scalar_mul(out=nmean.ap(), in0=sums.ap(),
                                                       scalar1=-1.0 / DM))
        W_("scalar", sv)
        for j in range(2):
            X_("scalar", lambda q, j=j: q.activation(
                out=scr_tm.ap()[:, j * DM:(j + 1) * DM],
                in_=xtm.ap()[:, j * DM:(j + 1) * DM], func=AF.Square,
                bias=nmean.ap()[:, j:j + 1], accum_out=ssq.ap()[:, j:j + 1]))
        W_("scalar", sv, cst_ready)
        I_("scalar", ss, lambda q: q.activation(out=sdv.ap(), in_=ssq.ap(), func=AF.Sqrt,
                                                scale=cst.ap()[:, 1:2], bias=cst.ap()[:, 0:1]))
        W_("vector", ss)
        X_("vector", lambda q: q.reciprocal(out=rsv.ap(), in_=sdv.ap()))
        X_("vector", lambda q: q.tensor_copy(out=spacebuf.ap()[:, 0:2], in_=sdv.ap()))
        for j in range(2):
            X_("vector", lambda q, j=j: q.tensor_scalar(
                out=scr_tm.ap()[:, j * DM:(j + 1) * DM],
                in0=xtm.ap()[:, j * DM:(j + 1) * DM],
                scalar1=nmean.ap()[:, j:j + 1], scalar2=rsv.ap()[:, j:j + 1],
                op0=ALU.add, op1=ALU.mult))
        N_("vector", sv)
        lnx_tm_done = sv.n

        if not own:
            xtm_free = lnx_tm_done

        if stage < 3:
            continue
        W_("vector", sg, xhat_spill)
        W_("tensor", sv, lnx_tm_done)
        W_("tensor", sq, consts_loaded)
        for j in range(2):
            for b_ in range(8):
                bank = pb[(j * 8 + b_) % 2]
                if j * 8 + b_ >= 2:
                    W_("tensor", sv)
                I_("tensor", st, lambda q, j=j, b_=b_, bank=bank: q.transpose(
                    bank.ap()[:, 0:128], scr_tm.ap()[:, j * DM + b_ * 128: j * DM + (b_ + 1) * 128],
                    ident_s.ap()))
                W_("vector", st)
                I_("vector", sv, lambda q, j=j, b_=b_, bank=bank: q.tensor_copy(
                    out=lnx_f.ap()[:, b_ * TC + j * 128: b_ * TC + (j + 1) * 128],
                    in_=bank.ap()[:, 0:128]))
        lnxf_done = sv.n

        if stage < 4:
            continue
        n_mg = 4 if own else 2
        W_("tensor", sv, lnxf_done)
        for mg in range(n_mg):
            for k in range(8):
                slot = (mg * 8 + k) % 4
                W_("sync", st, wbuf_free[slot])
                vw = dma("sync", sq, wbuf.ap()[:, slot * 1024:(slot + 1) * 1024],
                         W_inT[k * 128:(k + 1) * 128, mg * 1024:(mg + 1) * 1024])
                W_("tensor", sq, vw)
                for m in range(8):
                    X_("tensor", lambda q, slot=slot, m=m, k=k: q.matmul(
                        pb[m].ap()[:, 0:TC],
                        wbuf.ap()[:, slot * 1024 + m * 128: slot * 1024 + (m + 1) * 128],
                        lnx_f.ap()[:, k * TC:(k + 1) * TC],
                        start=(k == 0), stop=(k == 7)))
                N_("tensor", st)
                wbuf_free[slot] = st.n
            mm_done = st.n
            if mg < 2:
                W_("vector", st, mm_done)
                for m in range(8):
                    dc = mg * 8 + m
                    X_("vector", lambda q, m=m, dc=dc: q.tensor_copy(
                        out=xpb.ap()[:, dc * 260 + 3: dc * 260 + 3 + TC],
                        in_=pb[m].ap()[:, 0:TC]))
                N_("vector", sv)
                W_("tensor", sv)
            else:
                W_("scalar", sv, prev_y_done)
                W_("scalar", st, mm_done)
                for m in range(8):
                    zc = (mg - 2) * 8 + m
                    X_("scalar", lambda q, m=m, zc=zc: q.activation(
                        out=szb.ap()[:, zc * TC:(zc + 1) * TC], in_=pb[m].ap()[:, 0:TC],
                        func=AF.Silu))
                N_("scalar", ss)
                W_("tensor", ss)

        if stage < 5:
            continue
        W_("vector", st)
        for dc in range(16):
            X_("vector", lambda q, dc=dc: q.tensor_scalar_mul(
                out=ub.ap()[:, dc * TC:(dc + 1) * TC],
                in0=xpb.ap()[:, dc * 260 + 0: dc * 260 + TC],
                scalar1=convw_s.ap()[:, dc * 4: dc * 4 + 1]))
            for kk in range(1, 4):
                X_("vector", lambda q, dc=dc, kk=kk: q.scalar_tensor_tensor(
                    out=ub.ap()[:, dc * TC:(dc + 1) * TC],
                    in0=xpb.ap()[:, dc * 260 + kk: dc * 260 + kk + TC],
                    scalar=convw_s.ap()[:, dc * 4 + kk: dc * 4 + kk + 1],
                    in1=ub.ap()[:, dc * TC:(dc + 1) * TC],
                    op0=ALU.mult, op1=ALU.add))
        I_("vector", sv, lambda q: q.tensor_copy(out=xpb3[:, :, 0:3], in_=xpb3[:, :, 256:259]))
        conv_done = sv.n
        W_("scalar", sv, conv_done)
        for dc in range(16):
            X_("scalar", lambda q, dc=dc: q.activation(
                out=ub.ap()[:, dc * TC:(dc + 1) * TC], in_=ub.ap()[:, dc * TC:(dc + 1) * TC],
                func=AF.Silu, bias=convb_s.ap()[:, dc:dc + 1]))
        N_("scalar", ss)
        u_done = ss.n

        if stage < 6:
            continue
        W_("tensor", ss, u_done)
        for k in range(16):
            X_("tensor", lambda q, k=k: q.matmul(
                pb[0].ap()[0:64, 0:TC], wxT_s.ap()[:, k * 96: k * 96 + 64],
                ub.ap()[:, k * TC:(k + 1) * TC], start=(k == 0), stop=(k == 15)))
        for k in range(16):
            X_("tensor", lambda q, k=k: q.matmul(
                pb[1].ap()[0:32, 0:TC], wxT_s.ap()[:, k * 96 + 64: k * 96 + 96],
                ub.ap()[:, k * TC:(k + 1) * TC], start=(k == 0), stop=(k == 15)))
        N_("tensor", st)
        W_("vector", st)
        X_("vector", lambda q: q.tensor_copy(out=xdbl.ap(), in_=pb[0].ap()[0:64, 0:TC]))
        I_("vector", sv, lambda q: q.tensor_copy(out=bcb.ap(), in_=pb[1].ap()[0:32, 0:TC]))
        xdbl_done = sv.n

        if stage < 7:
            continue
        W_("tensor", sv, xdbl_done)
        for m in range(16):
            bank = pb[2 + (m % 2)]
            if m >= 2:
                W_("tensor", ss)
            X_("tensor", lambda q, m=m, bank=bank: q.matmul(
                bank.ap()[:, 0:TC], wdtT_s.ap()[:, m * 128:(m + 1) * 128],
                xdbl.ap(), start=True, stop=True))
            N_("tensor", st)
            W_("scalar", st)
            X_("scalar", lambda q, m=m, bank=bank: q.activation(
                out=deltab.ap()[:, m * TC:(m + 1) * TC], in_=bank.ap()[:, 0:TC],
                func=AF.Exp, bias=bdt_s.ap()[:, m:m + 1]))
            I_("scalar", ss, lambda q, m=m: q.activation(
                out=deltab.ap()[:, m * TC:(m + 1) * TC],
                in_=deltab.ap()[:, m * TC:(m + 1) * TC], func=AF.Ln, bias=1.0))
        delta_done = ss.n

        if stage < 8:
            continue
        W_("vector", ss, delta_done)
        for dc in range(16):
            X_("vector", lambda q, dc=dc: q.tensor_mul(
                out=dub.ap()[:, dc * TC:(dc + 1) * TC],
                in0=deltab.ap()[:, dc * TC:(dc + 1) * TC],
                in1=ub.ap()[:, dc * TC:(dc + 1) * TC]))
        N_("vector", sv)
        du_done = sv.n

        pend_carry = None
        group_inc = {}
        gidx = 0
        for n in range(NST):
            W_("tensor", sv, du_done)
            if n >= 1:
                W_("tensor", sv)
            X_("tensor", lambda q, n=n: q.matmul(
                pb[4].ap()[:, 0:TC], sel_s.ap()[:, n * 128:(n + 1) * 128], bcb.ap(),
                start=True, stop=True))
            if own:
                X_("tensor", lambda q, n=n: q.matmul(
                    pb[5].ap()[:, 0:TC], sel_s.ap()[:, (16 + n) * 128:(17 + n) * 128], bcb.ap(),
                    start=True, stop=True))
            N_("tensor", st)
            W_("vector", st)
            X_("vector", lambda q: q.tensor_copy(out=Bb.ap(), in_=pb[4].ap()[:, 0:TC]))
            if own:
                X_("vector", lambda q: q.tensor_copy(out=Cb.ap(), in_=pb[5].ap()[:, 0:TC]))
            X_("vector", lambda q: q.tensor_copy(out=spacebuf.ap(), in_=carry.ap()[:, 0:4]))
            N_("vector", sv)
            bc_ready = sv.n

            for dc in range(16):
                slot = dc % 2
                W_("scalar", sv, dA_free[slot])
                col = dc * 16 + n
                X_("scalar", lambda q, dc=dc, col=col, slot=slot: q.activation(
                    out=dAb.ap()[:, slot * TC:(slot + 1) * TC],
                    in_=deltab.ap()[:, dc * TC:(dc + 1) * TC], func=AF.Exp,
                    scale=abuf.ap()[:, col:col + 1]))
                N_("scalar", ss)
                dA_ready = ss.n
                rs_ = gidx % 4
                if gidx - 4 in group_inc:
                    W_("gpsimd", sv, group_inc[gidx - 4])
                W_("gpsimd", sv, bc_ready)
                I_("gpsimd", sp, lambda q, dc=dc, rs_=rs_: q.tensor_tensor(
                    out=dBub.ap()[:, rs_ * TC:(rs_ + 1) * TC],
                    in0=dub.ap()[:, dc * TC:(dc + 1) * TC], in1=Bb.ap(), op=ALU.mult))
                dbu_ready = sp.n
                W_("vector", ss, dA_ready)
                W_("vector", sp, dbu_ready)
                sslot = (n * 16 + dc) % 2
                X_("vector", lambda q, slot=slot, col=col, sslot=sslot, rs_=rs_: q.tensor_tensor_scan(
                    out=sst.ap()[:, sslot * TC:(sslot + 1) * TC],
                    data0=dAb.ap()[:, slot * TC:(slot + 1) * TC],
                    data1=dBub.ap()[:, rs_ * TC:(rs_ + 1) * TC],
                    initial=carry.ap()[:, col:col + 1],
                    op0=ALU.mult, op1=ALU.add))
                if pend_carry is not None:
                    pc, ps_ = pend_carry
                    X_("vector", lambda q, pc=pc, ps_=ps_: q.tensor_copy(
                        out=carry.ap()[:, pc:pc + 1],
                        in_=sst.ap()[:, ps_ * TC + TC - 1: ps_ * TC + TC]))
                pend_carry = (col, sslot)
                if own:
                    if n == 0:
                        X_("vector", lambda q, dc=dc, sslot=sslot: q.tensor_mul(
                            out=yac.ap()[:, dc * TC:(dc + 1) * TC],
                            in0=sst.ap()[:, sslot * TC:(sslot + 1) * TC], in1=Cb.ap()))
                    else:
                        X_("vector", lambda q, sslot=sslot: q.tensor_mul(
                            out=tmpv.ap(), in0=sst.ap()[:, sslot * TC:(sslot + 1) * TC],
                            in1=Cb.ap()))
                        X_("vector", lambda q, dc=dc: q.tensor_add(
                            out=yac.ap()[:, dc * TC:(dc + 1) * TC],
                            in0=yac.ap()[:, dc * TC:(dc + 1) * TC], in1=tmpv.ap()))
                N_("vector", sv)
                dA_free[slot] = sv.n
                group_inc[gidx] = sv.n
                gidx += 1

        if pend_carry is not None:
            X_("vector", lambda q: q.tensor_copy(out=spacebuf.ap(), in_=carry.ap()[:, 0:4]))
            pc, ps_ = pend_carry
            I_("vector", sv, lambda q, pc=pc, ps_=ps_: q.tensor_copy(
                out=carry.ap()[:, pc:pc + 1],
                in_=sst.ap()[:, ps_ * TC + TC - 1: ps_ * TC + TC]))

        if not own or stage < 9:
            continue

        for dc in range(16):
            X_("vector", lambda q, dc=dc: q.scalar_tensor_tensor(
                out=tmpv.ap(), in0=ub.ap()[:, dc * TC:(dc + 1) * TC],
                scalar=dprm_s.ap()[:, dc:dc + 1],
                in1=yac.ap()[:, dc * TC:(dc + 1) * TC], op0=ALU.mult, op1=ALU.add))
            X_("vector", lambda q, dc=dc: q.tensor_mul(
                out=yac.ap()[:, dc * TC:(dc + 1) * TC], in0=tmpv.ap(),
                in1=szb.ap()[:, dc * TC:(dc + 1) * TC]))
        N_("vector", sv)
        y_done = sv.n
        prev_y_done = y_done

        W_("tensor", sv, y_done)
        for k in range(16):
            slot = k % 4
            W_("sync", st, wbuf_free[slot])
            vw = dma("sync", sq, wbuf.ap()[:, slot * 1024:(slot + 1) * 1024],
                     W_outT[k * 128:(k + 1) * 128, :])
            W_("tensor", sq, vw)
            for ts in range(2):
                for nn in range(2):
                    X_("tensor", lambda q, k=k, ts=ts, nn=nn, slot=slot: q.matmul(
                        pb[ts * 2 + nn].ap(),
                        yac.ap()[:, k * TC + ts * 128: k * TC + (ts + 1) * 128],
                        wbuf.ap()[:, slot * 1024 + nn * 512: slot * 1024 + (nn + 1) * 512],
                        start=(k == 0), stop=(k == 15)))
            N_("tensor", st)
            wbuf_free[slot] = st.n
        wout_done = st.n
        W_("vector", st, wout_done)
        W_("vector", sg, x1_spill)
        for ts in range(2):
            for nn in range(2):
                X_("vector", lambda q, ts=ts, nn=nn: q.tensor_add(
                    out=x1tm.ap()[:, ts * DM + nn * 512: ts * DM + (nn + 1) * 512],
                    in0=xtm.ap()[:, ts * DM + nn * 512: ts * DM + (nn + 1) * 512],
                    in1=pb[ts * 2 + nn].ap()))
        N_("vector", sv)
        x1_done = sv.n
        xtm_free = x1_done
        W_("gpsimd", sv, x1_done)
        for j in range(2):
            x1_spill = dma("gpsimd", sg, x1_d[t0 + j * 128: t0 + (j + 1) * 128, :],
                           x1tm.ap()[:, j * DM:(j + 1) * DM])

        if stage < 10:
            continue
        W_("scalar", sv, x1_done)
        for j in range(2):
            X_("scalar", lambda q, j=j: q.activation(
                out=scr_tm.ap()[:, j * DM:(j + 1) * DM],
                in_=x1tm.ap()[:, j * DM:(j + 1) * DM], func=AF.Identity,
                accum_out=sums.ap()[:, j:j + 1]))
        N_("scalar", ss)
        W_("vector", ss)
        I_("vector", sv, lambda q: q.tensor_scalar_mul(out=nmean.ap(), in0=sums.ap(),
                                                       scalar1=-1.0 / DM))
        W_("scalar", sv)
        for j in range(2):
            X_("scalar", lambda q, j=j: q.activation(
                out=scr_tm.ap()[:, j * DM:(j + 1) * DM],
                in_=x1tm.ap()[:, j * DM:(j + 1) * DM], func=AF.Square,
                bias=nmean.ap()[:, j:j + 1], accum_out=ssq.ap()[:, j:j + 1]))
        W_("scalar", sv, cst_ready)
        I_("scalar", ss, lambda q: q.activation(out=sdv.ap(), in_=ssq.ap(), func=AF.Sqrt,
                                                scale=cst.ap()[:, 1:2], bias=cst.ap()[:, 0:1]))
        W_("vector", ss)
        X_("vector", lambda q: q.reciprocal(out=rsv.ap(), in_=sdv.ap()))
        X_("vector", lambda q: q.tensor_copy(out=spacebuf.ap()[:, 0:2], in_=sdv.ap()))
        for j in range(2):
            X_("vector", lambda q, j=j: q.tensor_scalar(
                out=scr_tm.ap()[:, j * DM:(j + 1) * DM], in0=x1tm.ap()[:, j * DM:(j + 1) * DM],
                scalar1=nmean.ap()[:, j:j + 1], scalar2=rsv.ap()[:, j:j + 1],
                op0=ALU.add, op1=ALU.mult))
        N_("vector", sv)
        xh2_done = sv.n
        W_("gpsimd", sv, xh2_done)
        for j in range(2):
            xhat_spill = dma("gpsimd", sg, xhtm_d[t0 + j * 128: t0 + (j + 1) * 128, :],
                             scr_tm.ap()[:, j * DM:(j + 1) * DM])
        W_("tensor", sv, xh2_done)
        for j in range(2):
            for b_ in range(8):
                bank = pb[(j * 8 + b_) % 2]
                if j * 8 + b_ >= 2:
                    W_("tensor", sv)
                I_("tensor", st, lambda q, j=j, b_=b_, bank=bank: q.transpose(
                    bank.ap()[:, 0:128], scr_tm.ap()[:, j * DM + b_ * 128: j * DM + (b_ + 1) * 128],
                    ident_s.ap()))
                W_("vector", st)
                I_("vector", sv, lambda q, j=j, b_=b_, bank=bank: q.tensor_copy(
                    out=lnx_f.ap()[:, b_ * TC + j * 128: b_ * TC + (j + 1) * 128],
                    in_=bank.ap()[:, 0:128]))
        xhf_done = sv.n

        if stage < 11:
            continue
        W_("vector", sg, xhat_spill)
        for dc in range(8):
            X_("vector", lambda q, dc=dc: q.tensor_scalar(
                out=scr_tm.ap()[:, dc * TC:(dc + 1) * TC],
                in0=lnx_f.ap()[:, dc * TC:(dc + 1) * TC],
                scalar1=rng_s.ap()[:, dc:dc + 1], scalar2=rnb_s.ap()[:, dc:dc + 1],
                op0=ALU.mult, op1=ALU.add))
        N_("vector", sv)
        normed_done = sv.n
        W_("tensor", sv, normed_done)
        for k in range(8):
            X_("tensor", lambda q, k=k: q.matmul(
                pb[6].ap()[0:8, 0:TC], rwT_s.ap()[:, k * 8:(k + 1) * 8],
                scr_tm.ap()[:, k * TC:(k + 1) * TC], start=(k == 0), stop=(k == 7)))
        N_("tensor", st)
        W_("vector", st)
        I_("vector", sv, lambda q: q.tensor_scalar_add(
            out=lt_f.ap(), in0=pb[6].ap()[0:8, 0:TC], scalar1=rb_s.ap()))
        logits_done = sv.n
        if dbg and ch == nch_pre:
            W_("gpsimd", sv, logits_done)
            dma("gpsimd", sg, lt_dbg, lt_f.ap())
        W_("tensor", sv, logits_done)
        for j in range(2):
            X_("tensor", lambda q, j=j: q.transpose(
                pb[7].ap()[:, j * 8:(j + 1) * 8], lt_f.ap()[:, j * 128:(j + 1) * 128],
                ident_s.ap()[0:8, 0:8]))
        N_("tensor", st)
        W_("vector", st)
        for j in range(2):
            X_("vector", lambda q, j=j: q.tensor_copy(
                out=lt_tm.ap()[:, j * 8:(j + 1) * 8], in_=pb[7].ap()[:, j * 8:(j + 1) * 8]))
        for j in range(2):
            X_("vector", lambda q, j=j: q.reduce_max(
                out=m1c.ap()[:, j:j + 1], in_=lt_tm.ap()[:, j * 8:(j + 1) * 8], axis=AX.X))
        for j in range(2):
            X_("vector", lambda q, j=j: q.tensor_single_scalar(
                out=ge1.ap()[:, j * 8:(j + 1) * 8], in_=lt_tm.ap()[:, j * 8:(j + 1) * 8],
                scalar=m1c.ap()[:, j:j + 1], op=ALU.is_ge))
        for j in range(2):
            X_("vector", lambda q, j=j: q.scalar_tensor_tensor(
                out=ge2.ap()[:, j * 8:(j + 1) * 8], in0=ge1.ap()[:, j * 8:(j + 1) * 8],
                scalar=-1e30, in1=lt_tm.ap()[:, j * 8:(j + 1) * 8],
                op0=ALU.mult, op1=ALU.add))
        for j in range(2):
            X_("vector", lambda q, j=j: q.reduce_max(
                out=m2c.ap()[:, j:j + 1], in_=ge2.ap()[:, j * 8:(j + 1) * 8], axis=AX.X))
        for j in range(2):
            X_("vector", lambda q, j=j: q.tensor_single_scalar(
                out=ge2.ap()[:, j * 8:(j + 1) * 8], in_=ge2.ap()[:, j * 8:(j + 1) * 8],
                scalar=m2c.ap()[:, j:j + 1], op=ALU.is_ge))
        X_("vector", lambda q: q.tensor_sub(out=dcol.ap(), in0=m1c.ap(), in1=m2c.ap()))
        N_("vector", sv)
        if dbg and ch == nch_pre:
            X_("vector", lambda q: q.tensor_copy(out=dbg_s.ap()[:, 0:8], in_=lt_tm.ap()[:, 0:8]))
            X_("vector", lambda q: q.tensor_copy(out=dbg_s.ap()[:, 8:9], in_=m1c.ap()[:, 0:1]))
            I_("vector", sv, lambda q: q.tensor_copy(out=dbg_s.ap()[:, 9:10], in_=m2c.ap()[:, 0:1]))
            W_("gpsimd", sv)
            dma("gpsimd", sg, t8_dbg, dbg_s.ap())
        W_("scalar", sv)
        X_("scalar", lambda q: q.activation(out=w1c.ap(), in_=dcol.ap(), func=AF.Sigmoid))
        X_("scalar", lambda q: q.activation(out=w2c.ap(), in_=dcol.ap(), func=AF.Sigmoid,
                                            scale=-1.0))
        N_("scalar", ss)
        W_("vector", ss)
        for j in range(2):
            tg = (ch - nch_pre) * 2 + j
            X_("vector", lambda q, tg=tg, j=j: q.tensor_scalar_mul(
                out=comb_s.ap()[:, tg * 8:(tg + 1) * 8], in0=ge1.ap()[:, j * 8:(j + 1) * 8],
                scalar1=w1c.ap()[:, j:j + 1]))
        for j in range(2):
            tg = (ch - nch_pre) * 2 + j
            X_("vector", lambda q, tg=tg, j=j: q.scalar_tensor_tensor(
                out=comb_s.ap()[:, tg * 8:(tg + 1) * 8], in0=ge2.ap()[:, j * 8:(j + 1) * 8],
                scalar=w2c.ap()[:, j:j + 1],
                in1=comb_s.ap()[:, tg * 8:(tg + 1) * 8], op0=ALU.mult, op1=ALU.add))
        N_("vector", sv)
        W_("tensor", sv)
        for j in range(2):
            tg = (ch - nch_pre) * 2 + j
            bank = pb[3 + j % 2]
            I_("tensor", st, lambda q, tg=tg, bank=bank: q.transpose(
                bank.ap()[0:8, 0:128], comb_s.ap()[:, tg * 8:(tg + 1) * 8],
                ident_s.ap()))
            W_("vector", st)
            I_("vector", sv, lambda q, tg=tg, bank=bank: q.tensor_copy(
                out=comb_f.ap()[:, tg * 128:(tg + 1) * 128], in_=bank.ap()[0:8, 0:128]))

    if dbg:
        W_("gpsimd", sv)
        dma("gpsimd", sg, comb_dbg, comb_s.ap())
    bar_t, bar_v, bar_s = st.n, sv.n, ss.n
    bar_q, bar_g = sq.n, sg.n
    for eng in ENGS:
        W_(eng, st, bar_t)
        W_(eng, sv, bar_v)
        W_(eng, ss, bar_s)
        W_(eng, sq, bar_q)
        W_(eng, sg, bar_g)

    CAP = 256
    _f = _Bump()

    def fsb(name, shape, dtype=F32):
        assert dtype is F32
        return _f.take(shape)

    xh_tm = fsb("xh_tm", (128, 4 * 1024))
    x1f = fsb("x1f", (128, 4 * 1024))
    Gcm = fsb("Gcm", (128, 2 * 512))
    Gw = fsb("Gw", (128, 2 * 512))
    Gtm = fsb("Gtm", (128, 4 * 256))
    hg_f = fsb("hg_f", (128, 8 * CAP))
    gg_f = fsb("gg_f", (128, 32 * CAP))
    og_tm = fsb("og_tm", (128, 2 * 1024))
    wbuf1 = fsb("wbuf1", (128, 4 * 512))
    wbuf2 = fsb("wbuf2", (128, 4 * 1024))
    tmpf = fsb("tmpf", (128, 512))
    M8 = fsb("M8", (8, 512))
    spcV = fsb("spcV", (128, 8))
    spcS = fsb("spcS", (8, 8))
    R8 = fsb("R8", (8, 512))
    ones8 = fsb("ones8", (8, 512))

    w1_free = [0, 0, 0, 0]
    w2_free = [0, 0, 0, 0]
    st_out_total = 0
    xh_free = 0
    x1f_free = 0

    X_("vector", lambda q: q.memset(ones8.ap(), 1.0))

    for fc in range(nfc):
        ft0 = fc * FTC
        W_("sync", sv, xh_free)
        W_("sync", sg, x1f_free)
        vxh = 0
        for ts in range(4):
            vxh = dma("sync", sq, xh_tm.ap()[:, ts * 1024:(ts + 1) * 1024],
                      xhtm_d[ft0 + ts * 128: ft0 + (ts + 1) * 128, :])
        vx1 = 0
        for ts in range(4):
            vx1 = dma("sync", sq, x1f.ap()[:, ts * 1024:(ts + 1) * 1024],
                      x1_d[ft0 + ts * 128: ft0 + (ts + 1) * 128, :])
        W_("vector", sq, vx1)

        X_("vector", lambda q, fc=fc: q.tensor_single_scalar(
            out=M8.ap(), in_=comb_f.ap()[:, fc * FTC:(fc + 1) * FTC], scalar=0.0,
            op=ALU.is_gt))
        X_("vector", lambda q: q.tensor_tensor_scan(
            out=R8.ap(), data0=ones8.ap(), data1=M8.ap(), initial=0.0,
            op0=ALU.mult, op1=ALU.add))
        X_("vector", lambda q: q.tensor_copy(out=spcV.ap()[0:8, 0:4], in_=M8.ap()[:, 0:4]))
        X_("vector", lambda q: q.tensor_mul(out=R8.ap(), in0=R8.ap(), in1=M8.ap()))
        X_("vector", lambda q: q.tensor_copy(out=spcV.ap()[0:8, :], in_=R8.ap()[:, 0:8]))
        N_("vector", sv)
        rank_done = sv.n

        for e in range(nexp):
            W_("tensor", sv, rank_done)
            if e > 0:
                W_("tensor", sv)
            X_("tensor", lambda q, e=e: q.matmul(
                pb[6].ap(), sel_s.ap()[0:8, e * 128:(e + 1) * 128], R8.ap(),
                start=True, stop=True))
            X_("tensor", lambda q, e=e, fc=fc: q.matmul(
                pb[7].ap(), sel_s.ap()[0:8, e * 128:(e + 1) * 128],
                comb_f.ap()[:, fc * FTC:(fc + 1) * FTC], start=True, stop=True))
            N_("tensor", st)
            W_("vector", st)
            for cc in range(2):
                X_("vector", lambda q, cc=cc: q.tensor_single_scalar(
                    out=Gcm.ap()[:, cc * 512:(cc + 1) * 512], in_=pb[6].ap(),
                    scalar=iota_s.ap()[:, cc:cc + 1], op=ALU.is_ge))
            for cc in range(2):
                X_("vector", lambda q, cc=cc: q.tensor_single_scalar(
                    out=tmpf.ap(), in_=pb[6].ap(),
                    scalar=iota_s.ap()[:, 2 + cc:3 + cc], op=ALU.is_ge))
                X_("vector", lambda q, cc=cc: q.tensor_sub(
                    out=Gcm.ap()[:, cc * 512:(cc + 1) * 512],
                    in0=Gcm.ap()[:, cc * 512:(cc + 1) * 512], in1=tmpf.ap()))
            for cc in range(2):
                X_("vector", lambda q, cc=cc: q.tensor_mul(
                    out=Gw.ap()[:, cc * 512:(cc + 1) * 512],
                    in0=Gcm.ap()[:, cc * 512:(cc + 1) * 512], in1=pb[7].ap()))
            X_("vector", lambda q: q.tensor_copy(out=spcV.ap(), in_=Gcm.ap()[:, 0:8]))
            N_("vector", sv)
            g_built = sv.n

            W_("tensor", sv, g_built)
            for cc in range(2):
                for tb in range(4):
                    bank = pb[(cc * 4 + tb) % 2]
                    if cc * 4 + tb >= 2:
                        W_("tensor", sv)
                    I_("tensor", st, lambda q, cc=cc, tb=tb, bank=bank: q.transpose(
                        bank.ap()[:, 0:128],
                        Gcm.ap()[:, cc * 512 + tb * 128: cc * 512 + (tb + 1) * 128],
                        ident_s.ap()))
                    W_("vector", st)
                    I_("vector", sv, lambda q, cc=cc, tb=tb, bank=bank: q.tensor_copy(
                        out=Gtm.ap()[:, tb * 256 + cc * 128: tb * 256 + (cc + 1) * 128],
                        in_=bank.ap()[:, 0:128]))
            X_("vector", lambda q: q.tensor_copy(out=spcV.ap(), in_=Gcm.ap()[:, 0:8]))
            N_("vector", sv)
            gtm_done = sv.n

            W_("tensor", sv, gtm_done)
            W_("tensor", sq, vxh)
            for tb in range(4):
                for cc in range(2):
                    for nn in range(2):
                        X_("tensor", lambda q, tb=tb, cc=cc, nn=nn: q.matmul(
                            pb[cc * 2 + nn].ap(),
                            Gtm.ap()[:, tb * 256 + cc * 128: tb * 256 + (cc + 1) * 128],
                            xh_tm.ap()[:, tb * 1024 + nn * 512: tb * 1024 + (nn + 1) * 512],
                            start=(tb == 0), stop=(tb == 3)))
            N_("tensor", st)
            gat_done = st.n
            W_("vector", st, gat_done)
            for cc in range(2):
                for nn in range(2):
                    X_("vector", lambda q, cc=cc, nn=nn: q.tensor_copy(
                        out=og_tm.ap()[:, cc * 1024 + nn * 512: cc * 1024 + (nn + 1) * 512],
                        in_=pb[cc * 2 + nn].ap()))
            X_("vector", lambda q: q.tensor_copy(out=spcV.ap(), in_=Gcm.ap()[:, 0:8]))
            N_("vector", sv)
            xg_done = sv.n

            W_("tensor", sv, xg_done)
            for cc in range(2):
                for db in range(8):
                    bank = pb[(cc * 8 + db) % 2 + 4]
                    if cc * 8 + db >= 2:
                        W_("vector", st)
                        W_("tensor", sv)
                    I_("tensor", st, lambda q, cc=cc, db=db, bank=bank: q.transpose(
                        bank.ap()[:, 0:128],
                        og_tm.ap()[:, cc * 1024 + db * 128: cc * 1024 + (db + 1) * 128],
                        ident_s.ap()))
                    W_("vector", st)
                    I_("vector", sv, lambda q, e=e, cc=cc, db=db, bank=bank: q.tensor_scalar(
                        out=hg_f.ap()[:, db * CAP + cc * 128: db * CAP + (cc + 1) * 128],
                        in0=bank.ap()[:, 0:128],
                        scalar1=lng_s.ap()[:, e * 8 + db: e * 8 + db + 1],
                        scalar2=lnb_s.ap()[:, e * 8 + db: e * 8 + db + 1],
                        op0=ALU.mult, op1=ALU.add))
            X_("vector", lambda q: q.tensor_copy(out=spcV.ap(), in_=Gcm.ap()[:, 0:8]))
            N_("vector", sv)
            hg_done = sv.n

            W_("tensor", sv, hg_done)
            for mgh in range(8):
                bb = (mgh % 2) * 4
                if mgh > 1:
                    W_("tensor", ss)
                for k in range(8):
                    slot = (mgh * 8 + k) % 4
                    W_("sync", st, w1_free[slot])
                    vw = dma("sync", sq, wbuf1.ap()[:, slot * 512:(slot + 1) * 512],
                             w1[e, k * 128:(k + 1) * 128, mgh * 512:(mgh + 1) * 512])
                    W_("tensor", sq, vw)
                    for m in range(4):
                        X_("tensor", lambda q, slot=slot, m=m, k=k, bb=bb: q.matmul(
                            pb[bb + m].ap()[:, 0:CAP],
                            wbuf1.ap()[:, slot * 512 + m * 128: slot * 512 + (m + 1) * 128],
                            hg_f.ap()[:, k * CAP:(k + 1) * CAP],
                            start=(k == 0), stop=(k == 7)))
                    N_("tensor", st)
                    w1_free[slot] = st.n
                mg_done = st.n
                W_("scalar", st, mg_done)
                for m in range(4):
                    ic = mgh * 4 + m
                    X_("scalar", lambda q, e=e, m=m, ic=ic, bb=bb: q.activation(
                        out=gg_f.ap()[:, ic * CAP:(ic + 1) * CAP], in_=pb[bb + m].ap()[:, 0:CAP],
                        func=AF.Gelu, bias=b1_s.ap()[:, e * 32 + ic: e * 32 + ic + 1]))
                N_("scalar", ss)
            g_done = ss.n

            W_("tensor", ss, g_done)
            for cc in range(2):
                for nn in range(2):
                    X_("tensor", lambda q, e=e, cc=cc, nn=nn: q.matmul(
                        pb[cc * 2 + nn].ap(), sel_s.ap()[0:8, e * 128:(e + 1) * 128],
                        b2_s.ap()[:, nn * 512:(nn + 1) * 512], start=True, stop=False))
            for k2 in range(32):
                slot = k2 % 4
                W_("sync", st, w2_free[slot])
                vw = dma("sync", sq, wbuf2.ap()[:, slot * 1024:(slot + 1) * 1024],
                         w2[e, k2 * 128:(k2 + 1) * 128, :])
                W_("tensor", sq, vw)
                for cc in range(2):
                    for nn in range(2):
                        X_("tensor", lambda q, k2=k2, cc=cc, nn=nn, slot=slot: q.matmul(
                            pb[cc * 2 + nn].ap(),
                            gg_f.ap()[:, k2 * CAP + cc * 128: k2 * CAP + (cc + 1) * 128],
                            wbuf2.ap()[:, slot * 1024 + nn * 512: slot * 1024 + (nn + 1) * 512],
                            start=False, stop=(k2 == 31)))
                N_("tensor", st)
                w2_free[slot] = st.n
            w2_done = st.n
            W_("scalar", st, w2_done)
            for cc in range(2):
                for nn in range(2):
                    X_("scalar", lambda q, cc=cc, nn=nn: q.copy(
                        out=og_tm.ap()[:, cc * 1024 + nn * 512: cc * 1024 + (nn + 1) * 512],
                        in_=pb[cc * 2 + nn].ap()))
            X_("scalar", lambda q: q.copy(out=spcS.ap(), in_=M8.ap()[:, 0:8]))
            N_("scalar", ss)
            og_done = ss.n

            W_("tensor", ss, og_done)
            for tb in range(4):
                for cc in range(2):
                    for nn in range(2):
                        X_("tensor", lambda q, tb=tb, cc=cc, nn=nn: q.matmul(
                            pb[tb * 2 + nn].ap(),
                            Gw.ap()[:, cc * 512 + tb * 128: cc * 512 + (tb + 1) * 128],
                            og_tm.ap()[:, cc * 1024 + nn * 512: cc * 1024 + (nn + 1) * 512],
                            start=(cc == 0), stop=(cc == 1)))
            N_("tensor", st)
            sc_done = st.n
            W_("vector", st, sc_done)
            for tb in range(4):
                for nn in range(2):
                    X_("vector", lambda q, tb=tb, nn=nn: q.tensor_add(
                        out=x1f.ap()[:, tb * 1024 + nn * 512: tb * 1024 + (nn + 1) * 512],
                        in0=x1f.ap()[:, tb * 1024 + nn * 512: tb * 1024 + (nn + 1) * 512],
                        in1=pb[tb * 2 + nn].ap()))
            N_("vector", sv)

        out_ready = sv.n
        xh_free = out_ready
        W_("gpsimd", sv, out_ready)
        for ts in range(4):
            st_out_total = dma("gpsimd", sg, out_d[ft0 + ts * 128: ft0 + (ts + 1) * 128, :],
                               x1f.ap()[:, ts * 1024:(ts + 1) * 1024])
        x1f_free = st_out_total


    if st_out_total == 0:
        for ts in range(4):
            st_out_total = dma("gpsimd", sg, out_d[ts * 128:(ts + 1) * 128, :],
                               arena.ap()[:, 0:1024])
    W_("gpsimd", sg, st_out_total)

    with nc.Block() as block:
        @block.sync
        def _(q):
            for f in ops["sync"]:
                f(q)

        @block.gpsimd
        def _(q):
            for f in ops["gpsimd"]:
                f(q)

        @block.tensor
        def _(q):
            for f in ops["tensor"]:
                f(q)

        @block.vector
        def _(q):
            for f in ops["vector"]:
                f(q)

        @block.scalar
        def _(q):
            for f in ops["scalar"]:
                f(q)

    nc.reset()
    stack.pop_all()
    return nc


def _host_layout(inputs):
    g = lambda k: np.ascontiguousarray(np.asarray(inputs[k], dtype=np.float32))
    x = g("L0_attn_x")
    W_in = g("L0_attn_W_in")
    conv_w = g("L0_attn_conv_w")
    conv_b = g("L0_attn_conv_b")
    W_x = g("L0_attn_W_x")
    W_dt = g("L0_attn_W_dt")
    b_dt = g("L0_attn_b_dt")
    A_log = g("L0_attn_A_log")
    D_param = g("L0_attn_D_param")
    W_out = g("L0_attn_W_out")
    rn_g = g("L0_ffn_rn_g")
    rn_b = g("L0_ffn_rn_b")
    router_w = g("L0_ffn_router_w")
    router_b = g("L0_ffn_router_b")
    ln_g = g("L0_ffn_ln_g")
    ln_b = g("L0_ffn_ln_b")
    w1 = g("L0_ffn_w1")
    b1 = g("L0_ffn_b1")
    w2 = g("L0_ffn_w2")
    b2 = g("L0_ffn_b2")

    C = np.ascontiguousarray
    shared = {
        "W_inT": C(W_in.T),
        "W_xT": C(W_x.T),
        "W_dtT": C(W_dt.T),
        "W_outT": C(W_out.T),
        "convw": C(conv_w[:, 0, :].reshape(16, 128, 4).transpose(1, 0, 2).reshape(128, 64)),
        "convb": C(conv_b.reshape(16, 128).T),
        "bdt": C(b_dt.reshape(16, 128).T),
        "alog": C(A_log.reshape(16, 128, 16).transpose(1, 0, 2).reshape(128, 256)),
        "dprm": C(D_param.reshape(16, 128).T),
        "rng_g": C(rn_g.reshape(8, 128).T),
        "rnb_g": C(rn_b.reshape(8, 128).T),
        "rwT": C(router_w.T),
        "rbias": C(router_b.reshape(8, 1)),
        "lng": C(ln_g.reshape(8, 8, 128).transpose(2, 0, 1).reshape(128, 64)),
        "lnb": C(ln_b.reshape(8, 8, 128).transpose(2, 0, 1).reshape(128, 64)),
        "w1": w1,
        "b1l": C(b1.reshape(8, 32, 128).transpose(2, 0, 1).reshape(128, 256)),
        "w2": w2,
        "b2m": b2,
        "ident": np.eye(128, dtype=np.float32),
        "sel": C(np.kron(np.eye(32, dtype=np.float32), np.ones((1, 128), np.float32))),
        "iota": C(np.stack([np.arange(1, 129), np.arange(129, 257),
                            np.arange(2, 130), np.arange(130, 258)],
                           axis=1).astype(np.float32)),
    }
    in_maps = []
    for c in range(8):
        b, h = c // 2, c % 2
        m = dict(shared)
        m["x_pre"] = C(x[b, 0:TOWN, :])
        m["x_own"] = C(x[b, h * TOWN:(h + 1) * TOWN, :])
        m["flag"] = np.full((128, 1), float(h), dtype=np.float32)
        in_maps.append(m)
    return in_maps


def kernel(**inputs):
    in_maps = _host_layout(inputs)
    nc = build_nc()
    res = run_bass_kernel_spmd(nc, in_maps, core_ids=list(range(8)))
    out = np.empty((B, L, DM), dtype=np.float32)
    for c in range(8):
        b, h = c // 2, c % 2
        out[b, h * TOWN:(h + 1) * TOWN, :] = res.results[c]["out"]
    return out


# version 08 (final)
# speedup vs baseline: 1.3127x; 1.0095x over previous
import numpy as np
import concourse.bass as bass
import concourse.mybir as mybir
from concourse.bass_utils import run_bass_kernel_spmd

F32 = mybir.dt.float32
AF = mybir.ActivationFunctionType
ALU = mybir.AluOpType
AX = mybir.AxisListType

B, L, DM = 4, 4096, 1024
DI = 2048
NST = 16
DTR = 64
E, II = 8, 4096
TOWN = 2048
TC = 256
NCH_PRE, NCH_OWN = 8, 8
FTC = 512
NFC = TOWN // FTC
EPS = 1e-5

ENGS = ("sync", "gpsimd", "tensor", "vector", "scalar")


class Sem:
    def __init__(self, h):
        self.h = h
        self.n = 0


def build_nc(nch_pre=NCH_PRE, nch_own=NCH_OWN, nfc=NFC, stage=99, dbg=False, nexp=E):
    import contextlib
    nc = bass.Bass("TRN2", target_bir_lowering=False, detect_race_conditions=False)
    stack = contextlib.ExitStack()

    d_in = {}
    def din(name, shape):
        d_in[name] = nc.dram_tensor(name, list(shape), F32, kind="ExternalInput").ap()
        return d_in[name]

    x_pre = din("x_pre", (TOWN, DM))
    x_own = din("x_own", (TOWN, DM))
    flag = din("flag", (128, 1))
    W_inT = din("W_inT", (DM, 2 * DI))
    W_xT = din("W_xT", (DI, 96))
    W_dtT = din("W_dtT", (DTR, DI))
    W_outT = din("W_outT", (DI, DM))
    convw = din("convw", (128, 64))
    convb = din("convb", (128, 16))
    bdt = din("bdt", (128, 16))
    alog = din("alog", (128, 256))
    dprm = din("dprm", (128, 16))
    rng_g = din("rng_g", (128, 8))
    rnb_g = din("rnb_g", (128, 8))
    rwT = din("rwT", (DM, E))
    rbias = din("rbias", (E, 1))
    lng = din("lng", (128, 64))
    lnb = din("lnb", (128, 64))
    w1 = din("w1", (E, DM, II))
    b1l = din("b1l", (128, 256))
    w2 = din("w2", (E, II, DM))
    b2m = din("b2m", (E, DM))
    ident = din("ident", (128, 128))
    sel = din("sel", (32, 32 * 128))
    iota = din("iota", (128, 4))

    out_d = nc.dram_tensor("out", [TOWN, DM], F32, kind="ExternalOutput").ap()
    _ik = "ExternalOutput" if dbg else "Internal"
    g_dbg = nc.dram_tensor("g_dbg", [128, 32 * FTC], F32, kind=_ik).ap()
    comb_dbg = nc.dram_tensor("comb_dbg", [128, 16 * 8], F32, kind=_ik).ap()
    lt_dbg = nc.dram_tensor("lt_dbg", [8, 256], F32, kind=_ik).ap()
    t8_dbg = nc.dram_tensor("t8_dbg", [128, 16], F32, kind=_ik).ap()
    hf_dbg = nc.dram_tensor("hf_dbg", [128, 8 * FTC], F32, kind=_ik).ap()
    xhtm_d = nc.dram_tensor("xhtm_d", [TOWN, DM], F32, kind=_ik).ap()
    x1_d = nc.dram_tensor("x1_d", [TOWN, DM], F32, kind=_ik).ap()

    sq = Sem(stack.enter_context(nc.semaphore("sq")))
    sp = Sem(stack.enter_context(nc.semaphore("sp")))
    sg = Sem(stack.enter_context(nc.semaphore("sg")))
    st = Sem(stack.enter_context(nc.semaphore("st")))
    sv = Sem(stack.enter_context(nc.semaphore("sv")))
    ss = Sem(stack.enter_context(nc.semaphore("ss")))

    ops = {e: [] for e in ENGS}

    def E_(eng, fn):
        ops[eng].append(fn)

    def W_(eng, sem, val=None):
        v = sem.n if val is None else val
        if v <= 0:
            return
        E_(eng, lambda q, h=sem.h, vv=v: q.wait_ge(h, vv))

    def I_(eng, sem, fn, k=1):
        sem.n += k
        E_(eng, lambda q, f=fn, h=sem.h, kk=k: f(q).then_inc(h, kk))

    def X_(eng, fn):
        E_(eng, fn)

    def NS_(eng, sem):
        if eng == "scalar":
            E_(eng, lambda q: q.copy(out=spc_sc.ap(), in_=flag_s.ap()))
        else:
            E_(eng, lambda q: q.tensor_copy(out=spc_ve.ap(), in_=flag_s.ap()))
        N_(eng, sem)

    def N_(eng, sem, k=1):
        sem.n += k
        last = ops[eng].pop()
        ops[eng].append(lambda q, f=last, h=sem.h, kk=k: f(q).then_inc(h, kk))

    def dma(queue, sem, out_ap, in_ap):
        I_(queue, sem, lambda q, o=out_ap, i=in_ap: q.dma_start(out=o, in_=i), 16)
        return sem.n

    def sb(name, shape, dtype=F32):
        return stack.enter_context(nc.sbuf_tensor(name, list(shape), dtype))

    ident_s = sb("ident_s", (128, 128))
    sel_s = sb("sel_s", (32, 32 * 128))
    wxT_s = sb("wxT_s", (128, 16 * 96))
    wdtT_s = sb("wdtT_s", (64, DI))
    rwT_s = sb("rwT_s", (128, 64))
    abuf = sb("abuf", (128, 256))
    convw_s = sb("convw_s", (128, 64))
    convb_s = sb("convb_s", (128, 16))
    bdt_s = sb("bdt_s", (128, 16))
    dprm_s = sb("dprm_s", (128, 16))
    rng_s = sb("rng_s", (128, 8))
    rnb_s = sb("rnb_s", (128, 8))
    lng_s = sb("lng_s", (128, 64))
    lnb_s = sb("lnb_s", (128, 64))
    b1_s = sb("b1_s", (128, 256))
    b2_s = sb("b2_s", (E, DM))
    rb_s = sb("rb_s", (8, 1))
    flag_s = sb("flag_s", (128, 1))
    ones1 = sb("ones1", (1, 128))
    spc_sc = sb("spc_sc", (128, 1))
    spc_ve = sb("spc_ve", (128, 1))
    cst = sb("cst", (128, 2))
    comb_s = sb("comb_s", (128, 16 * 8))
    comb_f = sb("comb_f", (8, TOWN))
    iota_s = sb("iota_s", (128, 4))
    dbg_s = sb("dbg_s", (128, 16))

    pb = [stack.enter_context(nc.psum_tensor(f"pb{i}", [128, 512], F32)) for i in range(8)]

    for dst, src in [(ident_s, ident), (sel_s, sel), (wdtT_s, W_dtT),
                     (convw_s, convw), (convb_s, convb), (bdt_s, bdt),
                     (dprm_s, dprm), (rng_s, rng_g), (rnb_s, rnb_g),
                     (lng_s, lng), (lnb_s, lnb), (b1_s, b1l), (b2_s, b2m),
                     (rb_s, rbias), (flag_s, flag), (iota_s, iota)]:
        dma("sync", sq, dst.ap(), src)
    for k in range(16):
        dma("sync", sq, wxT_s.ap()[:, k * 96:(k + 1) * 96], W_xT[k * 128:(k + 1) * 128, :])
    for k in range(8):
        dma("sync", sq, rwT_s.ap()[:, k * 8:(k + 1) * 8], rwT[k * 128:(k + 1) * 128, :])
    v_alog = dma("sync", sq, abuf.ap(), alog)
    consts_loaded = sq.n
    W_("scalar", sq, v_alog)
    X_("scalar", lambda q: q.activation(out=abuf.ap(), in_=abuf.ap(), func=AF.Exp))
    NS_("scalar", ss)
    W_("vector", ss)
    I_("vector", sv, lambda q: q.tensor_scalar_mul(out=abuf.ap(), in0=abuf.ap(), scalar1=-1.0))
    X_("vector", lambda q: q.memset(ones1.ap(), 1.0))
    X_("vector", lambda q: q.memset(cst.ap()[:, 0:1], EPS))
    I_("vector", sv, lambda q: q.memset(cst.ap()[:, 1:2], 1.0 / DM))
    abuf_ready = sv.n
    cst_ready = sv.n

    ARENA_W = 41168
    arena = sb("arena", (128, ARENA_W))

    class _Buf:
        __slots__ = ("_ap",)
        def __init__(self, ap_):
            self._ap = ap_
        def ap(self):
            return self._ap

    class _Bump:
        def __init__(self):
            self.off = 0
        def take(self, shape):
            assert shape[0] <= 128
            w = int(np.prod(shape[1:]))
            w_al = (w + 15) // 16 * 16
            ap_ = arena.ap()[0:shape[0], self.off:self.off + w]
            self.off += w_al
            assert self.off <= ARENA_W, f"arena overflow {self.off}"
            return _Buf(ap_)

    _m = _Bump()

    def msb(name, shape, dtype=F32):
        assert dtype is F32
        return _m.take(shape)

    xtm = msb("xtm", (128, 2 * DM))
    scr_tm = msb("scr_tm", (128, 2 * DM))
    x1tm = msb("x1tm", (128, 2 * DM))
    lnx_f = msb("lnx_f", (128, 8 * TC))
    wbuf = msb("wbuf", (128, 4 * 1024))
    xpb = msb("xpb", (128, 16 * 260))
    szb = msb("szb", (128, 16 * TC))
    ub = msb("ub", (128, 16 * TC))
    deltab = msb("deltab", (128, 16 * TC))
    dub = msb("dub", (128, 16 * TC))
    yac = msb("yac", (128, 16 * TC))
    xdbl = msb("xdbl", (64, TC))
    bcb = msb("bcb", (32, TC))
    Bb = msb("Bb", (128, TC))
    Cb = msb("Cb", (128, TC))
    dAb = msb("dAb", (128, 2 * TC))
    dBub = msb("dBub", (128, 4 * TC))
    sst = msb("sst", (128, 2 * TC))
    spacebuf = msb("spacebuf", (128, 4))
    tmpv = msb("tmpv", (128, TC))
    carry = msb("carry", (128, 256))
    sums = msb("sums", (128, 2))
    nmean = msb("nmean", (128, 2))
    ssq = msb("ssq", (128, 2))
    sdv = msb("sdv", (128, 2))
    rsv = msb("rsv", (128, 2))
    lt_f = msb("lt_f", (8, TC))
    lt_tm = msb("lt_tm", (128, 16))
    top8 = msb("top8", (128, 8))
    m1c = msb("m1c", (128, 2))
    m2c = msb("m2c", (128, 2))
    dcol = msb("dcol", (128, 2))
    w1c = msb("w1c", (128, 2))
    w2c = msb("w2c", (128, 2))
    w12 = msb("w12", (128, 1))
    ge1 = msb("ge1", (128, 16))
    ge2 = msb("ge2", (128, 16))

    W_("vector", sq, consts_loaded)
    X_("vector", lambda q: q.memset(carry.ap(), 0.0))
    xpb3 = xpb.ap().rearrange("p (c w) -> p c w", w=260)
    X_("vector", lambda q: q.memset(xpb3[:, :, 0:3], 0.0))
    I_("vector", sv, lambda q: q.memset(yac.ap(), 0.0))

    wbuf_free = [0, 0, 0, 0]
    dA_free = [abuf_ready, abuf_ready]
    xtm_free = 0
    x1_spill = 0
    xhat_spill = 0
    prev_y_done = 0

    for ch in range(nch_pre + nch_own):
        own = ch >= nch_pre
        t0 = (ch - nch_pre) * TC if own else (ch * TC) % TOWN
        xsrc = x_own if own else x_pre

        if ch == nch_pre:
            W_("vector", sq, consts_loaded)
            X_("vector", lambda q: q.tensor_copy(out=spacebuf.ap(), in_=carry.ap()[:, 4:8]))
            X_("vector", lambda q: q.tensor_copy(out=spacebuf.ap(), in_=carry.ap()[:, 8:12]))
            X_("vector", lambda q: q.tensor_scalar_mul(out=carry.ap(), in0=carry.ap(),
                                                       scalar1=flag_s.ap()))
            tails = xpb3[:, :, 0:3]
            I_("vector", sv, lambda q, t=tails: q.tensor_scalar_mul(out=t, in0=t,
                                                                    scalar1=flag_s.ap()))

        if stage < 2:
            continue
        W_("sync", sv, xtm_free)
        vx = 0
        for j in range(2):
            vx = dma("sync", sq, xtm.ap()[:, j * DM:(j + 1) * DM],
                     xsrc[t0 + j * 128: t0 + (j + 1) * 128, :])

        W_("scalar", sq, vx)
        for j in range(2):
            X_("scalar", lambda q, j=j: q.activation(
                out=scr_tm.ap()[:, j * DM:(j + 1) * DM],
                in_=xtm.ap()[:, j * DM:(j + 1) * DM], func=AF.Identity,
                accum_out=sums.ap()[:, j:j + 1]))
        NS_("scalar", ss)
        W_("vector", ss)
        X_("vector", lambda q: q.tensor_scalar_mul(out=nmean.ap(), in0=sums.ap(),
                                                    scalar1=-1.0 / DM))
        NS_("vector", sv)
        W_("scalar", sv)
        for j in range(2):
            X_("scalar", lambda q, j=j: q.activation(
                out=scr_tm.ap()[:, j * DM:(j + 1) * DM],
                in_=xtm.ap()[:, j * DM:(j + 1) * DM], func=AF.Square,
                bias=nmean.ap()[:, j:j + 1], accum_out=ssq.ap()[:, j:j + 1]))
        W_("scalar", sv, cst_ready)
        X_("scalar", lambda q: q.activation(out=sdv.ap(), in_=ssq.ap(), func=AF.Sqrt,
                                              scale=cst.ap()[:, 1:2], bias=cst.ap()[:, 0:1]))
        NS_("scalar", ss)
        W_("vector", ss)
        X_("vector", lambda q: q.reciprocal(out=rsv.ap(), in_=sdv.ap()))
        X_("vector", lambda q: q.tensor_copy(out=spacebuf.ap()[:, 0:2], in_=sdv.ap()))
        for j in range(2):
            X_("vector", lambda q, j=j: q.tensor_scalar(
                out=scr_tm.ap()[:, j * DM:(j + 1) * DM],
                in0=xtm.ap()[:, j * DM:(j + 1) * DM],
                scalar1=nmean.ap()[:, j:j + 1], scalar2=rsv.ap()[:, j:j + 1],
                op0=ALU.add, op1=ALU.mult))
        N_("vector", sv)
        lnx_tm_done = sv.n

        if not own:
            xtm_free = lnx_tm_done

        if stage < 3:
            continue
        W_("vector", sg, xhat_spill)
        W_("tensor", sv, lnx_tm_done)
        W_("tensor", sq, consts_loaded)
        for j in range(2):
            for b_ in range(8):
                bank = pb[(j * 8 + b_) % 2]
                if j * 8 + b_ >= 2:
                    W_("tensor", sv)
                I_("tensor", st, lambda q, j=j, b_=b_, bank=bank: q.transpose(
                    bank.ap()[:, 0:128], scr_tm.ap()[:, j * DM + b_ * 128: j * DM + (b_ + 1) * 128],
                    ident_s.ap()))
                W_("vector", st)
                I_("vector", sv, lambda q, j=j, b_=b_, bank=bank: q.tensor_copy(
                    out=lnx_f.ap()[:, b_ * TC + j * 128: b_ * TC + (j + 1) * 128],
                    in_=bank.ap()[:, 0:128]))
        lnxf_done = sv.n

        if stage < 4:
            continue
        n_mg = 4 if own else 2
        W_("tensor", sv, lnxf_done)
        for mg in range(n_mg):
            for k in range(8):
                slot = (mg * 8 + k) % 4
                W_("sync", st, wbuf_free[slot])
                vw = dma("sync", sq, wbuf.ap()[:, slot * 1024:(slot + 1) * 1024],
                         W_inT[k * 128:(k + 1) * 128, mg * 1024:(mg + 1) * 1024])
                W_("tensor", sq, vw)
                for m in range(8):
                    X_("tensor", lambda q, slot=slot, m=m, k=k: q.matmul(
                        pb[m].ap()[:, 0:TC],
                        wbuf.ap()[:, slot * 1024 + m * 128: slot * 1024 + (m + 1) * 128],
                        lnx_f.ap()[:, k * TC:(k + 1) * TC],
                        start=(k == 0), stop=(k == 7)))
                N_("tensor", st)
                wbuf_free[slot] = st.n
            mm_done = st.n
            if mg < 2:
                W_("vector", st, mm_done)
                for m in range(8):
                    dc = mg * 8 + m
                    X_("vector", lambda q, m=m, dc=dc: q.tensor_copy(
                        out=xpb.ap()[:, dc * 260 + 3: dc * 260 + 3 + TC],
                        in_=pb[m].ap()[:, 0:TC]))
                N_("vector", sv)
                W_("tensor", sv)
            else:
                W_("scalar", sv, prev_y_done)
                W_("scalar", st, mm_done)
                for m in range(8):
                    zc = (mg - 2) * 8 + m
                    X_("scalar", lambda q, m=m, zc=zc: q.activation(
                        out=szb.ap()[:, zc * TC:(zc + 1) * TC], in_=pb[m].ap()[:, 0:TC],
                        func=AF.Silu))
                N_("scalar", ss)
                W_("tensor", ss)

        if stage < 5:
            continue
        W_("vector", st)
        for dc in range(16):
            X_("vector", lambda q, dc=dc: q.tensor_scalar_mul(
                out=ub.ap()[:, dc * TC:(dc + 1) * TC],
                in0=xpb.ap()[:, dc * 260 + 0: dc * 260 + TC],
                scalar1=convw_s.ap()[:, dc * 4: dc * 4 + 1]))
            for kk in range(1, 4):
                X_("vector", lambda q, dc=dc, kk=kk: q.scalar_tensor_tensor(
                    out=ub.ap()[:, dc * TC:(dc + 1) * TC],
                    in0=xpb.ap()[:, dc * 260 + kk: dc * 260 + kk + TC],
                    scalar=convw_s.ap()[:, dc * 4 + kk: dc * 4 + kk + 1],
                    in1=ub.ap()[:, dc * TC:(dc + 1) * TC],
                    op0=ALU.mult, op1=ALU.add))
        I_("vector", sv, lambda q: q.tensor_copy(out=xpb3[:, :, 0:3], in_=xpb3[:, :, 256:259]))
        conv_done = sv.n
        W_("scalar", sv, conv_done)
        for dc in range(16):
            X_("scalar", lambda q, dc=dc: q.activation(
                out=ub.ap()[:, dc * TC:(dc + 1) * TC], in_=ub.ap()[:, dc * TC:(dc + 1) * TC],
                func=AF.Silu, bias=convb_s.ap()[:, dc:dc + 1]))
        N_("scalar", ss)
        u_done = ss.n

        if stage < 6:
            continue
        W_("tensor", ss, u_done)
        for k in range(16):
            X_("tensor", lambda q, k=k: q.matmul(
                pb[0].ap()[0:64, 0:TC], wxT_s.ap()[:, k * 96: k * 96 + 64],
                ub.ap()[:, k * TC:(k + 1) * TC], start=(k == 0), stop=(k == 15)))
        for k in range(16):
            X_("tensor", lambda q, k=k: q.matmul(
                pb[1].ap()[0:32, 0:TC], wxT_s.ap()[:, k * 96 + 64: k * 96 + 96],
                ub.ap()[:, k * TC:(k + 1) * TC], start=(k == 0), stop=(k == 15)))
        N_("tensor", st)
        W_("vector", st)
        X_("vector", lambda q: q.tensor_copy(out=xdbl.ap(), in_=pb[0].ap()[0:64, 0:TC]))
        X_("vector", lambda q: q.tensor_copy(out=bcb.ap(), in_=pb[1].ap()[0:32, 0:TC]))
        NS_("vector", sv)
        xdbl_done = sv.n

        if stage < 7:
            continue
        W_("tensor", sv, xdbl_done)
        for m in range(16):
            bank = pb[2 + (m % 2)]
            if m >= 2:
                W_("tensor", ss)
            X_("tensor", lambda q, m=m, bank=bank: q.matmul(
                bank.ap()[:, 0:TC], wdtT_s.ap()[:, m * 128:(m + 1) * 128],
                xdbl.ap(), start=True, stop=True))
            N_("tensor", st)
            W_("scalar", st)
            X_("scalar", lambda q, m=m, bank=bank: q.activation(
                out=deltab.ap()[:, m * TC:(m + 1) * TC], in_=bank.ap()[:, 0:TC],
                func=AF.Exp, bias=bdt_s.ap()[:, m:m + 1]))
            I_("scalar", ss, lambda q, m=m: q.activation(
                out=deltab.ap()[:, m * TC:(m + 1) * TC],
                in_=deltab.ap()[:, m * TC:(m + 1) * TC], func=AF.Ln, bias=1.0))
        delta_done = ss.n

        if stage < 8:
            continue
        W_("vector", ss, delta_done)
        for dc in range(16):
            X_("vector", lambda q, dc=dc: q.tensor_mul(
                out=dub.ap()[:, dc * TC:(dc + 1) * TC],
                in0=deltab.ap()[:, dc * TC:(dc + 1) * TC],
                in1=ub.ap()[:, dc * TC:(dc + 1) * TC]))
        N_("vector", sv)
        du_done = sv.n

        pend_carry = None
        group_inc = {}
        gidx = 0
        for n in range(NST):
            W_("tensor", sv, du_done)
            if n >= 1:
                W_("tensor", sv)
            X_("tensor", lambda q, n=n: q.matmul(
                pb[4].ap()[:, 0:TC], sel_s.ap()[:, n * 128:(n + 1) * 128], bcb.ap(),
                start=True, stop=True))
            if own:
                X_("tensor", lambda q, n=n: q.matmul(
                    pb[5].ap()[:, 0:TC], sel_s.ap()[:, (16 + n) * 128:(17 + n) * 128], bcb.ap(),
                    start=True, stop=True))
            N_("tensor", st)
            W_("vector", st)
            X_("vector", lambda q: q.tensor_copy(out=Bb.ap(), in_=pb[4].ap()[:, 0:TC]))
            if own:
                X_("vector", lambda q: q.tensor_copy(out=Cb.ap(), in_=pb[5].ap()[:, 0:TC]))
            X_("vector", lambda q: q.tensor_copy(out=spacebuf.ap(), in_=carry.ap()[:, 0:4]))
            N_("vector", sv)
            bc_ready = sv.n

            for dc in range(16):
                slot = dc % 2
                W_("scalar", sv, dA_free[slot])
                col = dc * 16 + n
                X_("scalar", lambda q, dc=dc, col=col, slot=slot: q.activation(
                    out=dAb.ap()[:, slot * TC:(slot + 1) * TC],
                    in_=deltab.ap()[:, dc * TC:(dc + 1) * TC], func=AF.Exp,
                    scale=abuf.ap()[:, col:col + 1]))
                N_("scalar", ss)
                dA_ready = ss.n
                rs_ = gidx % 4
                if gidx - 4 in group_inc:
                    W_("gpsimd", sv, group_inc[gidx - 4])
                W_("gpsimd", sv, bc_ready)
                I_("gpsimd", sp, lambda q, dc=dc, rs_=rs_: q.tensor_tensor(
                    out=dBub.ap()[:, rs_ * TC:(rs_ + 1) * TC],
                    in0=dub.ap()[:, dc * TC:(dc + 1) * TC], in1=Bb.ap(), op=ALU.mult))
                dbu_ready = sp.n
                W_("vector", ss, dA_ready)
                W_("vector", sp, dbu_ready)
                sslot = (n * 16 + dc) % 2
                X_("vector", lambda q, slot=slot, col=col, sslot=sslot, rs_=rs_: q.tensor_tensor_scan(
                    out=sst.ap()[:, sslot * TC:(sslot + 1) * TC],
                    data0=dAb.ap()[:, slot * TC:(slot + 1) * TC],
                    data1=dBub.ap()[:, rs_ * TC:(rs_ + 1) * TC],
                    initial=carry.ap()[:, col:col + 1],
                    op0=ALU.mult, op1=ALU.add))
                if pend_carry is not None:
                    if not own:
                        X_("vector", lambda q: q.tensor_copy(
                            out=spacebuf.ap(), in_=dBub.ap()[:, 0:4]))
                    pc, ps_ = pend_carry
                    X_("vector", lambda q, pc=pc, ps_=ps_: q.tensor_copy(
                        out=carry.ap()[:, pc:pc + 1],
                        in_=sst.ap()[:, ps_ * TC + TC - 1: ps_ * TC + TC]))
                pend_carry = (col, sslot)
                if own:
                    if n == 0:
                        X_("vector", lambda q, dc=dc, sslot=sslot: q.tensor_mul(
                            out=yac.ap()[:, dc * TC:(dc + 1) * TC],
                            in0=sst.ap()[:, sslot * TC:(sslot + 1) * TC], in1=Cb.ap()))
                    else:
                        X_("vector", lambda q, sslot=sslot: q.tensor_mul(
                            out=tmpv.ap(), in0=sst.ap()[:, sslot * TC:(sslot + 1) * TC],
                            in1=Cb.ap()))
                        X_("vector", lambda q, dc=dc: q.tensor_add(
                            out=yac.ap()[:, dc * TC:(dc + 1) * TC],
                            in0=yac.ap()[:, dc * TC:(dc + 1) * TC], in1=tmpv.ap()))
                N_("vector", sv)
                dA_free[slot] = sv.n
                group_inc[gidx] = sv.n
                gidx += 1

        if pend_carry is not None:
            X_("vector", lambda q: q.tensor_copy(out=spacebuf.ap(), in_=carry.ap()[:, 0:4]))
            pc, ps_ = pend_carry
            I_("vector", sv, lambda q, pc=pc, ps_=ps_: q.tensor_copy(
                out=carry.ap()[:, pc:pc + 1],
                in_=sst.ap()[:, ps_ * TC + TC - 1: ps_ * TC + TC]))

        if not own or stage < 9:
            continue

        for dc in range(16):
            X_("vector", lambda q, dc=dc: q.scalar_tensor_tensor(
                out=tmpv.ap(), in0=ub.ap()[:, dc * TC:(dc + 1) * TC],
                scalar=dprm_s.ap()[:, dc:dc + 1],
                in1=yac.ap()[:, dc * TC:(dc + 1) * TC], op0=ALU.mult, op1=ALU.add))
            X_("vector", lambda q, dc=dc: q.tensor_mul(
                out=yac.ap()[:, dc * TC:(dc + 1) * TC], in0=tmpv.ap(),
                in1=szb.ap()[:, dc * TC:(dc + 1) * TC]))
        N_("vector", sv)
        y_done = sv.n
        prev_y_done = y_done

        W_("tensor", sv, y_done)
        for k in range(16):
            slot = k % 4
            W_("sync", st, wbuf_free[slot])
            vw = dma("sync", sq, wbuf.ap()[:, slot * 1024:(slot + 1) * 1024],
                     W_outT[k * 128:(k + 1) * 128, :])
            W_("tensor", sq, vw)
            for ts in range(2):
                for nn in range(2):
                    X_("tensor", lambda q, k=k, ts=ts, nn=nn, slot=slot: q.matmul(
                        pb[ts * 2 + nn].ap(),
                        yac.ap()[:, k * TC + ts * 128: k * TC + (ts + 1) * 128],
                        wbuf.ap()[:, slot * 1024 + nn * 512: slot * 1024 + (nn + 1) * 512],
                        start=(k == 0), stop=(k == 15)))
            N_("tensor", st)
            wbuf_free[slot] = st.n
        wout_done = st.n
        W_("vector", st, wout_done)
        W_("vector", sg, x1_spill)
        for ts in range(2):
            for nn in range(2):
                X_("vector", lambda q, ts=ts, nn=nn: q.tensor_add(
                    out=x1tm.ap()[:, ts * DM + nn * 512: ts * DM + (nn + 1) * 512],
                    in0=xtm.ap()[:, ts * DM + nn * 512: ts * DM + (nn + 1) * 512],
                    in1=pb[ts * 2 + nn].ap()))
        NS_("vector", sv)
        x1_done = sv.n
        xtm_free = x1_done
        W_("gpsimd", sv, x1_done)
        for j in range(2):
            x1_spill = dma("gpsimd", sg, x1_d[t0 + j * 128: t0 + (j + 1) * 128, :],
                           x1tm.ap()[:, j * DM:(j + 1) * DM])

        if stage < 10:
            continue
        W_("scalar", sv, x1_done)
        for j in range(2):
            X_("scalar", lambda q, j=j: q.activation(
                out=scr_tm.ap()[:, j * DM:(j + 1) * DM],
                in_=x1tm.ap()[:, j * DM:(j + 1) * DM], func=AF.Identity,
                accum_out=sums.ap()[:, j:j + 1]))
        NS_("scalar", ss)
        W_("vector", ss)
        X_("vector", lambda q: q.tensor_scalar_mul(out=nmean.ap(), in0=sums.ap(),
                                                    scalar1=-1.0 / DM))
        NS_("vector", sv)
        W_("scalar", sv)
        for j in range(2):
            X_("scalar", lambda q, j=j: q.activation(
                out=scr_tm.ap()[:, j * DM:(j + 1) * DM],
                in_=x1tm.ap()[:, j * DM:(j + 1) * DM], func=AF.Square,
                bias=nmean.ap()[:, j:j + 1], accum_out=ssq.ap()[:, j:j + 1]))
        W_("scalar", sv, cst_ready)
        X_("scalar", lambda q: q.activation(out=sdv.ap(), in_=ssq.ap(), func=AF.Sqrt,
                                              scale=cst.ap()[:, 1:2], bias=cst.ap()[:, 0:1]))
        NS_("scalar", ss)
        W_("vector", ss)
        X_("vector", lambda q: q.reciprocal(out=rsv.ap(), in_=sdv.ap()))
        X_("vector", lambda q: q.tensor_copy(out=spacebuf.ap()[:, 0:2], in_=sdv.ap()))
        for j in range(2):
            X_("vector", lambda q, j=j: q.tensor_scalar(
                out=scr_tm.ap()[:, j * DM:(j + 1) * DM], in0=x1tm.ap()[:, j * DM:(j + 1) * DM],
                scalar1=nmean.ap()[:, j:j + 1], scalar2=rsv.ap()[:, j:j + 1],
                op0=ALU.add, op1=ALU.mult))
        NS_("vector", sv)
        xh2_done = sv.n
        W_("gpsimd", sv, xh2_done)
        for j in range(2):
            xhat_spill = dma("gpsimd", sg, xhtm_d[t0 + j * 128: t0 + (j + 1) * 128, :],
                             scr_tm.ap()[:, j * DM:(j + 1) * DM])
        W_("tensor", sv, xh2_done)
        for j in range(2):
            for b_ in range(8):
                bank = pb[(j * 8 + b_) % 2]
                if j * 8 + b_ >= 2:
                    W_("tensor", sv)
                I_("tensor", st, lambda q, j=j, b_=b_, bank=bank: q.transpose(
                    bank.ap()[:, 0:128], scr_tm.ap()[:, j * DM + b_ * 128: j * DM + (b_ + 1) * 128],
                    ident_s.ap()))
                W_("vector", st)
                I_("vector", sv, lambda q, j=j, b_=b_, bank=bank: q.tensor_copy(
                    out=lnx_f.ap()[:, b_ * TC + j * 128: b_ * TC + (j + 1) * 128],
                    in_=bank.ap()[:, 0:128]))
        xhf_done = sv.n

        if stage < 11:
            continue
        W_("vector", sg, xhat_spill)
        for dc in range(8):
            X_("vector", lambda q, dc=dc: q.tensor_scalar(
                out=scr_tm.ap()[:, dc * TC:(dc + 1) * TC],
                in0=lnx_f.ap()[:, dc * TC:(dc + 1) * TC],
                scalar1=rng_s.ap()[:, dc:dc + 1], scalar2=rnb_s.ap()[:, dc:dc + 1],
                op0=ALU.mult, op1=ALU.add))
        N_("vector", sv)
        normed_done = sv.n
        W_("tensor", sv, normed_done)
        for k in range(8):
            X_("tensor", lambda q, k=k: q.matmul(
                pb[6].ap()[0:8, 0:TC], rwT_s.ap()[:, k * 8:(k + 1) * 8],
                scr_tm.ap()[:, k * TC:(k + 1) * TC], start=(k == 0), stop=(k == 7)))
        N_("tensor", st)
        W_("vector", st)
        X_("vector", lambda q: q.tensor_scalar_add(
            out=lt_f.ap(), in0=pb[6].ap()[0:8, 0:TC], scalar1=rb_s.ap()))
        NS_("vector", sv)
        logits_done = sv.n
        if dbg and ch == nch_pre:
            W_("gpsimd", sv, logits_done)
            dma("gpsimd", sg, lt_dbg, lt_f.ap())
        W_("tensor", sv, logits_done)
        for j in range(2):
            X_("tensor", lambda q, j=j: q.transpose(
                pb[7].ap()[:, j * 8:(j + 1) * 8], lt_f.ap()[:, j * 128:(j + 1) * 128],
                ident_s.ap()[0:8, 0:8]))
        N_("tensor", st)
        W_("vector", st)
        for j in range(2):
            X_("vector", lambda q, j=j: q.tensor_copy(
                out=lt_tm.ap()[:, j * 8:(j + 1) * 8], in_=pb[7].ap()[:, j * 8:(j + 1) * 8]))
        for j in range(2):
            X_("vector", lambda q, j=j: q.reduce_max(
                out=m1c.ap()[:, j:j + 1], in_=lt_tm.ap()[:, j * 8:(j + 1) * 8], axis=AX.X))
        X_("vector", lambda q: q.tensor_copy(out=spc_ve.ap(), in_=flag_s.ap()))
        for j in range(2):
            X_("vector", lambda q, j=j: q.tensor_single_scalar(
                out=ge1.ap()[:, j * 8:(j + 1) * 8], in_=lt_tm.ap()[:, j * 8:(j + 1) * 8],
                scalar=m1c.ap()[:, j:j + 1], op=ALU.is_ge))
        for j in range(2):
            X_("vector", lambda q, j=j: q.scalar_tensor_tensor(
                out=ge2.ap()[:, j * 8:(j + 1) * 8], in0=ge1.ap()[:, j * 8:(j + 1) * 8],
                scalar=-1e30, in1=lt_tm.ap()[:, j * 8:(j + 1) * 8],
                op0=ALU.mult, op1=ALU.add))
        for j in range(2):
            X_("vector", lambda q, j=j: q.reduce_max(
                out=m2c.ap()[:, j:j + 1], in_=ge2.ap()[:, j * 8:(j + 1) * 8], axis=AX.X))
        X_("vector", lambda q: q.tensor_copy(out=spc_ve.ap(), in_=flag_s.ap()))
        for j in range(2):
            X_("vector", lambda q, j=j: q.tensor_single_scalar(
                out=ge2.ap()[:, j * 8:(j + 1) * 8], in_=ge2.ap()[:, j * 8:(j + 1) * 8],
                scalar=m2c.ap()[:, j:j + 1], op=ALU.is_ge))
        X_("vector", lambda q: q.tensor_sub(out=dcol.ap(), in0=m1c.ap(), in1=m2c.ap()))
        NS_("vector", sv)
        if dbg and ch == nch_pre:
            X_("vector", lambda q: q.tensor_copy(out=dbg_s.ap()[:, 0:8], in_=lt_tm.ap()[:, 0:8]))
            X_("vector", lambda q: q.tensor_copy(out=dbg_s.ap()[:, 8:9], in_=m1c.ap()[:, 0:1]))
            I_("vector", sv, lambda q: q.tensor_copy(out=dbg_s.ap()[:, 9:10], in_=m2c.ap()[:, 0:1]))
            W_("gpsimd", sv)
            dma("gpsimd", sg, t8_dbg, dbg_s.ap())
        W_("scalar", sv)
        X_("scalar", lambda q: q.activation(out=w1c.ap(), in_=dcol.ap(), func=AF.Sigmoid))
        X_("scalar", lambda q: q.activation(out=w2c.ap(), in_=dcol.ap(), func=AF.Sigmoid,
                                            scale=-1.0))
        NS_("scalar", ss)
        W_("vector", ss)
        for j in range(2):
            tg = (ch - nch_pre) * 2 + j
            X_("vector", lambda q, tg=tg, j=j: q.tensor_scalar_mul(
                out=comb_s.ap()[:, tg * 8:(tg + 1) * 8], in0=ge1.ap()[:, j * 8:(j + 1) * 8],
                scalar1=w1c.ap()[:, j:j + 1]))
        for j in range(2):
            tg = (ch - nch_pre) * 2 + j
            X_("vector", lambda q, tg=tg, j=j: q.scalar_tensor_tensor(
                out=comb_s.ap()[:, tg * 8:(tg + 1) * 8], in0=ge2.ap()[:, j * 8:(j + 1) * 8],
                scalar=w2c.ap()[:, j:j + 1],
                in1=comb_s.ap()[:, tg * 8:(tg + 1) * 8], op0=ALU.mult, op1=ALU.add))
        NS_("vector", sv)
        W_("tensor", sv)
        for j in range(2):
            tg = (ch - nch_pre) * 2 + j
            bank = pb[3 + j % 2]
            I_("tensor", st, lambda q, tg=tg, bank=bank: q.transpose(
                bank.ap()[0:8, 0:128], comb_s.ap()[:, tg * 8:(tg + 1) * 8],
                ident_s.ap()))
            W_("vector", st)
            I_("vector", sv, lambda q, tg=tg, bank=bank: q.tensor_copy(
                out=comb_f.ap()[:, tg * 128:(tg + 1) * 128], in_=bank.ap()[0:8, 0:128]))

    if dbg:
        W_("gpsimd", sv)
        dma("gpsimd", sg, comb_dbg, comb_s.ap())
    bar_t, bar_v, bar_s = st.n, sv.n, ss.n
    bar_q, bar_g = sq.n, sg.n
    for eng in ENGS:
        W_(eng, st, bar_t)
        W_(eng, sv, bar_v)
        W_(eng, ss, bar_s)
        W_(eng, sq, bar_q)
        W_(eng, sg, bar_g)

    CAP = 256
    _f = _Bump()

    def fsb(name, shape, dtype=F32):
        assert dtype is F32
        return _f.take(shape)

    xh_tm = fsb("xh_tm", (128, 4 * 1024))
    x1f = fsb("x1f", (128, 4 * 1024))
    Gcm = fsb("Gcm", (128, 2 * 512))
    Gw = fsb("Gw", (128, 2 * 512))
    Gtm = fsb("Gtm", (128, 4 * 256))
    hg_f = fsb("hg_f", (128, 8 * CAP))
    gg_f = fsb("gg_f", (128, 32 * CAP))
    og_tm = fsb("og_tm", (128, 2 * 1024))
    wbuf1 = fsb("wbuf1", (128, 4 * 512))
    wbuf2 = fsb("wbuf2", (128, 4 * 1024))
    tmpf = fsb("tmpf", (128, 512))
    M8 = fsb("M8", (8, 512))
    spcV = fsb("spcV", (128, 8))
    spcS = fsb("spcS", (8, 8))
    R8 = fsb("R8", (8, 512))
    ones8 = fsb("ones8", (8, 512))

    w1_free = [0, 0, 0, 0]
    w2_free = [0, 0, 0, 0]
    st_out_total = 0
    xh_free = 0
    x1f_free = 0

    X_("vector", lambda q: q.memset(ones8.ap(), 1.0))

    for fc in range(nfc):
        ft0 = fc * FTC
        W_("sync", sv, xh_free)
        W_("sync", sg, x1f_free)
        vxh = 0
        for ts in range(4):
            vxh = dma("sync", sq, xh_tm.ap()[:, ts * 1024:(ts + 1) * 1024],
                      xhtm_d[ft0 + ts * 128: ft0 + (ts + 1) * 128, :])
        vx1 = 0
        for ts in range(4):
            vx1 = dma("sync", sq, x1f.ap()[:, ts * 1024:(ts + 1) * 1024],
                      x1_d[ft0 + ts * 128: ft0 + (ts + 1) * 128, :])
        W_("vector", sq, vx1)

        X_("vector", lambda q, fc=fc: q.tensor_single_scalar(
            out=M8.ap(), in_=comb_f.ap()[:, fc * FTC:(fc + 1) * FTC], scalar=0.0,
            op=ALU.is_gt))
        X_("vector", lambda q: q.tensor_tensor_scan(
            out=R8.ap(), data0=ones8.ap(), data1=M8.ap(), initial=0.0,
            op0=ALU.mult, op1=ALU.add))
        X_("vector", lambda q: q.tensor_copy(out=spcV.ap()[0:8, 0:4], in_=M8.ap()[:, 0:4]))
        X_("vector", lambda q: q.tensor_mul(out=R8.ap(), in0=R8.ap(), in1=M8.ap()))
        X_("vector", lambda q: q.tensor_copy(out=spcV.ap()[0:8, :], in_=R8.ap()[:, 0:8]))
        N_("vector", sv)
        rank_done = sv.n

        for e in range(nexp):
            W_("tensor", sv, rank_done)
            if e > 0:
                W_("tensor", sv)
            X_("tensor", lambda q, e=e: q.matmul(
                pb[6].ap(), sel_s.ap()[0:8, e * 128:(e + 1) * 128], R8.ap(),
                start=True, stop=True))
            X_("tensor", lambda q, e=e, fc=fc: q.matmul(
                pb[7].ap(), sel_s.ap()[0:8, e * 128:(e + 1) * 128],
                comb_f.ap()[:, fc * FTC:(fc + 1) * FTC], start=True, stop=True))
            N_("tensor", st)
            W_("vector", st)
            for cc in range(2):
                X_("vector", lambda q, cc=cc: q.tensor_single_scalar(
                    out=Gcm.ap()[:, cc * 512:(cc + 1) * 512], in_=pb[6].ap(),
                    scalar=iota_s.ap()[:, cc:cc + 1], op=ALU.is_ge))
            for cc in range(2):
                X_("vector", lambda q, cc=cc: q.tensor_single_scalar(
                    out=tmpf.ap(), in_=pb[6].ap(),
                    scalar=iota_s.ap()[:, 2 + cc:3 + cc], op=ALU.is_ge))
                X_("vector", lambda q, cc=cc: q.tensor_sub(
                    out=Gcm.ap()[:, cc * 512:(cc + 1) * 512],
                    in0=Gcm.ap()[:, cc * 512:(cc + 1) * 512], in1=tmpf.ap()))
            for cc in range(2):
                X_("vector", lambda q, cc=cc: q.tensor_mul(
                    out=Gw.ap()[:, cc * 512:(cc + 1) * 512],
                    in0=Gcm.ap()[:, cc * 512:(cc + 1) * 512], in1=pb[7].ap()))
            X_("vector", lambda q: q.tensor_copy(out=spcV.ap(), in_=Gcm.ap()[:, 0:8]))
            N_("vector", sv)
            g_built = sv.n

            W_("tensor", sv, g_built)
            for cc in range(2):
                for tb in range(4):
                    bank = pb[(cc * 4 + tb) % 2]
                    if cc * 4 + tb >= 2:
                        W_("tensor", sv)
                    I_("tensor", st, lambda q, cc=cc, tb=tb, bank=bank: q.transpose(
                        bank.ap()[:, 0:128],
                        Gcm.ap()[:, cc * 512 + tb * 128: cc * 512 + (tb + 1) * 128],
                        ident_s.ap()))
                    W_("vector", st)
                    I_("vector", sv, lambda q, cc=cc, tb=tb, bank=bank: q.tensor_copy(
                        out=Gtm.ap()[:, tb * 256 + cc * 128: tb * 256 + (cc + 1) * 128],
                        in_=bank.ap()[:, 0:128]))
            X_("vector", lambda q: q.tensor_copy(out=spcV.ap(), in_=Gcm.ap()[:, 0:8]))
            N_("vector", sv)
            gtm_done = sv.n

            W_("tensor", sv, gtm_done)
            W_("tensor", sq, vxh)
            for tb in range(4):
                for cc in range(2):
                    for nn in range(2):
                        X_("tensor", lambda q, tb=tb, cc=cc, nn=nn: q.matmul(
                            pb[cc * 2 + nn].ap(),
                            Gtm.ap()[:, tb * 256 + cc * 128: tb * 256 + (cc + 1) * 128],
                            xh_tm.ap()[:, tb * 1024 + nn * 512: tb * 1024 + (nn + 1) * 512],
                            start=(tb == 0), stop=(tb == 3)))
            N_("tensor", st)
            gat_done = st.n
            W_("vector", st, gat_done)
            for cc in range(2):
                for nn in range(2):
                    X_("vector", lambda q, cc=cc, nn=nn: q.tensor_copy(
                        out=og_tm.ap()[:, cc * 1024 + nn * 512: cc * 1024 + (nn + 1) * 512],
                        in_=pb[cc * 2 + nn].ap()))
            X_("vector", lambda q: q.tensor_copy(out=spcV.ap(), in_=Gcm.ap()[:, 0:8]))
            N_("vector", sv)
            xg_done = sv.n

            W_("tensor", sv, xg_done)
            for cc in range(2):
                for db in range(8):
                    bank = pb[(cc * 8 + db) % 2 + 4]
                    if cc * 8 + db >= 2:
                        W_("vector", st)
                        W_("tensor", sv)
                    I_("tensor", st, lambda q, cc=cc, db=db, bank=bank: q.transpose(
                        bank.ap()[:, 0:128],
                        og_tm.ap()[:, cc * 1024 + db * 128: cc * 1024 + (db + 1) * 128],
                        ident_s.ap()))
                    W_("vector", st)
                    I_("vector", sv, lambda q, e=e, cc=cc, db=db, bank=bank: q.tensor_scalar(
                        out=hg_f.ap()[:, db * CAP + cc * 128: db * CAP + (cc + 1) * 128],
                        in0=bank.ap()[:, 0:128],
                        scalar1=lng_s.ap()[:, e * 8 + db: e * 8 + db + 1],
                        scalar2=lnb_s.ap()[:, e * 8 + db: e * 8 + db + 1],
                        op0=ALU.mult, op1=ALU.add))
            X_("vector", lambda q: q.tensor_copy(out=spcV.ap(), in_=Gcm.ap()[:, 0:8]))
            N_("vector", sv)
            hg_done = sv.n

            W_("tensor", sv, hg_done)
            for mgh in range(8):
                bb = (mgh % 2) * 4
                if mgh > 1:
                    W_("tensor", ss)
                for k in range(8):
                    slot = (mgh * 8 + k) % 4
                    W_("sync", st, w1_free[slot])
                    vw = dma("sync", sq, wbuf1.ap()[:, slot * 512:(slot + 1) * 512],
                             w1[e, k * 128:(k + 1) * 128, mgh * 512:(mgh + 1) * 512])
                    W_("tensor", sq, vw)
                    for m in range(4):
                        X_("tensor", lambda q, slot=slot, m=m, k=k, bb=bb: q.matmul(
                            pb[bb + m].ap()[:, 0:CAP],
                            wbuf1.ap()[:, slot * 512 + m * 128: slot * 512 + (m + 1) * 128],
                            hg_f.ap()[:, k * CAP:(k + 1) * CAP],
                            start=(k == 0), stop=(k == 7)))
                    N_("tensor", st)
                    w1_free[slot] = st.n
                mg_done = st.n
                W_("scalar", st, mg_done)
                for m in range(4):
                    ic = mgh * 4 + m
                    X_("scalar", lambda q, e=e, m=m, ic=ic, bb=bb: q.activation(
                        out=gg_f.ap()[:, ic * CAP:(ic + 1) * CAP], in_=pb[bb + m].ap()[:, 0:CAP],
                        func=AF.Gelu, bias=b1_s.ap()[:, e * 32 + ic: e * 32 + ic + 1]))
                N_("scalar", ss)
            g_done = ss.n

            W_("tensor", ss, g_done)
            for cc in range(2):
                for nn in range(2):
                    X_("tensor", lambda q, e=e, cc=cc, nn=nn: q.matmul(
                        pb[cc * 2 + nn].ap(), sel_s.ap()[0:8, e * 128:(e + 1) * 128],
                        b2_s.ap()[:, nn * 512:(nn + 1) * 512], start=True, stop=False))
            for k2 in range(32):
                slot = k2 % 4
                W_("sync", st, w2_free[slot])
                vw = dma("sync", sq, wbuf2.ap()[:, slot * 1024:(slot + 1) * 1024],
                         w2[e, k2 * 128:(k2 + 1) * 128, :])
                W_("tensor", sq, vw)
                for cc in range(2):
                    for nn in range(2):
                        X_("tensor", lambda q, k2=k2, cc=cc, nn=nn, slot=slot: q.matmul(
                            pb[cc * 2 + nn].ap(),
                            gg_f.ap()[:, k2 * CAP + cc * 128: k2 * CAP + (cc + 1) * 128],
                            wbuf2.ap()[:, slot * 1024 + nn * 512: slot * 1024 + (nn + 1) * 512],
                            start=False, stop=(k2 == 31)))
                N_("tensor", st)
                w2_free[slot] = st.n
            w2_done = st.n
            W_("scalar", st, w2_done)
            for cc in range(2):
                for nn in range(2):
                    X_("scalar", lambda q, cc=cc, nn=nn: q.copy(
                        out=og_tm.ap()[:, cc * 1024 + nn * 512: cc * 1024 + (nn + 1) * 512],
                        in_=pb[cc * 2 + nn].ap()))
            X_("scalar", lambda q: q.copy(out=spcS.ap(), in_=M8.ap()[:, 0:8]))
            N_("scalar", ss)
            og_done = ss.n

            W_("tensor", ss, og_done)
            for tb in range(4):
                for cc in range(2):
                    for nn in range(2):
                        X_("tensor", lambda q, tb=tb, cc=cc, nn=nn: q.matmul(
                            pb[tb * 2 + nn].ap(),
                            Gw.ap()[:, cc * 512 + tb * 128: cc * 512 + (tb + 1) * 128],
                            og_tm.ap()[:, cc * 1024 + nn * 512: cc * 1024 + (nn + 1) * 512],
                            start=(cc == 0), stop=(cc == 1)))
            N_("tensor", st)
            sc_done = st.n
            W_("vector", st, sc_done)
            for tb in range(4):
                for nn in range(2):
                    X_("vector", lambda q, tb=tb, nn=nn: q.tensor_add(
                        out=x1f.ap()[:, tb * 1024 + nn * 512: tb * 1024 + (nn + 1) * 512],
                        in0=x1f.ap()[:, tb * 1024 + nn * 512: tb * 1024 + (nn + 1) * 512],
                        in1=pb[tb * 2 + nn].ap()))
            N_("vector", sv)

        out_ready = sv.n
        xh_free = out_ready
        W_("gpsimd", sv, out_ready)
        for ts in range(4):
            st_out_total = dma("gpsimd", sg, out_d[ft0 + ts * 128: ft0 + (ts + 1) * 128, :],
                               x1f.ap()[:, ts * 1024:(ts + 1) * 1024])
        x1f_free = st_out_total


    if st_out_total == 0:
        for ts in range(4):
            st_out_total = dma("gpsimd", sg, out_d[ts * 128:(ts + 1) * 128, :],
                               arena.ap()[:, 0:1024])
    W_("gpsimd", sg, st_out_total)

    with nc.Block() as block:
        @block.sync
        def _(q):
            for f in ops["sync"]:
                f(q)

        @block.gpsimd
        def _(q):
            for f in ops["gpsimd"]:
                f(q)

        @block.tensor
        def _(q):
            for f in ops["tensor"]:
                f(q)

        @block.vector
        def _(q):
            for f in ops["vector"]:
                f(q)

        @block.scalar
        def _(q):
            for f in ops["scalar"]:
                f(q)

    nc.reset()
    stack.pop_all()
    return nc


def _host_layout(inputs):
    g = lambda k: np.ascontiguousarray(np.asarray(inputs[k], dtype=np.float32))
    x = g("L0_attn_x")
    W_in = g("L0_attn_W_in")
    conv_w = g("L0_attn_conv_w")
    conv_b = g("L0_attn_conv_b")
    W_x = g("L0_attn_W_x")
    W_dt = g("L0_attn_W_dt")
    b_dt = g("L0_attn_b_dt")
    A_log = g("L0_attn_A_log")
    D_param = g("L0_attn_D_param")
    W_out = g("L0_attn_W_out")
    rn_g = g("L0_ffn_rn_g")
    rn_b = g("L0_ffn_rn_b")
    router_w = g("L0_ffn_router_w")
    router_b = g("L0_ffn_router_b")
    ln_g = g("L0_ffn_ln_g")
    ln_b = g("L0_ffn_ln_b")
    w1 = g("L0_ffn_w1")
    b1 = g("L0_ffn_b1")
    w2 = g("L0_ffn_w2")
    b2 = g("L0_ffn_b2")

    C = np.ascontiguousarray
    shared = {
        "W_inT": C(W_in.T),
        "W_xT": C(W_x.T),
        "W_dtT": C(W_dt.T),
        "W_outT": C(W_out.T),
        "convw": C(conv_w[:, 0, :].reshape(16, 128, 4).transpose(1, 0, 2).reshape(128, 64)),
        "convb": C(conv_b.reshape(16, 128).T),
        "bdt": C(b_dt.reshape(16, 128).T),
        "alog": C(A_log.reshape(16, 128, 16).transpose(1, 0, 2).reshape(128, 256)),
        "dprm": C(D_param.reshape(16, 128).T),
        "rng_g": C(rn_g.reshape(8, 128).T),
        "rnb_g": C(rn_b.reshape(8, 128).T),
        "rwT": C(router_w.T),
        "rbias": C(router_b.reshape(8, 1)),
        "lng": C(ln_g.reshape(8, 8, 128).transpose(2, 0, 1).reshape(128, 64)),
        "lnb": C(ln_b.reshape(8, 8, 128).transpose(2, 0, 1).reshape(128, 64)),
        "w1": w1,
        "b1l": C(b1.reshape(8, 32, 128).transpose(2, 0, 1).reshape(128, 256)),
        "w2": w2,
        "b2m": b2,
        "ident": np.eye(128, dtype=np.float32),
        "sel": C(np.kron(np.eye(32, dtype=np.float32), np.ones((1, 128), np.float32))),
        "iota": C(np.stack([np.arange(1, 129), np.arange(129, 257),
                            np.arange(2, 130), np.arange(130, 258)],
                           axis=1).astype(np.float32)),
    }
    in_maps = []
    for c in range(8):
        b, h = c // 2, c % 2
        m = dict(shared)
        m["x_pre"] = C(x[b, 0:TOWN, :])
        m["x_own"] = C(x[b, h * TOWN:(h + 1) * TOWN, :])
        m["flag"] = np.full((128, 1), float(h), dtype=np.float32)
        in_maps.append(m)
    return in_maps


def kernel(**inputs):
    in_maps = _host_layout(inputs)
    nc = build_nc()
    res = run_bass_kernel_spmd(nc, in_maps, core_ids=list(range(8)))
    out = np.empty((B, L, DM), dtype=np.float32)
    for c in range(8):
        b, h = c // 2, c % 2
        out[b, h * TOWN:(h + 1) * TOWN, :] = res.results[c]["out"]
    return out
